# version 11
# baseline (speedup 1.0000x reference)
import math
import os as _os
import numpy as np
K_DBG = int(_os.environ.get("K_DBG", "0"))
from contextlib import ExitStack

import concourse.bass as bass
import concourse.mybir as mybir
from concourse.bass_utils import run_bass_kernel_spmd

F32 = mybir.dt.float32
BF16 = mybir.dt.bfloat16
AF = mybir.ActivationFunctionType
ALU = mybir.AluOpType
AX = mybir.AxisListType

T = 2048
D = 1024
NT = 16
NTS = 4
FC = 8
NE = 18
IC = 4
BIG = 1.0e30
LN_EPS = 1e-5
RMS_EPS = 1e-6

SCALAR_CONSTS = [LN_EPS, RMS_EPS, 1e-20, 0.5, -0.5, -1.0, math.log(1.25), 0.02]

SLOT_COLS = 4096 * 3
WBLOB_PER = D * 512 * 3

CF_GB = 0
CF_QG = 16
CF_KG = 17
CF_INDQ = 18
CF_INDK = CF_INDQ + 128
CF_SEL = CF_INDK + 64
CF_I16 = CF_SEL + 512
CF_COLS = CF_I16 + 16

CB_ID = 0
CB_COLS = 128

MI_QG8 = 0
MI_MQ = 1
MI_MK = 2
MI_MH = 3
MI_NEGM = 8
MI_NEGMC = 24

LN_SUMA, LN_SUMB, LN_SQA, LN_SQB, LN_INV, LN_B, LN_A2, LN_B2, LN_MU, LN_VAR, LN_T1, LN_T2 = (
    0, 16, 32, 48, 64, 80, 96, 112, 128, 144, 160, 176)


class Prog:
    def __init__(self):
        self.nc = bass.Bass("TRN2", target_bir_lowering=False)
        self.es = ExitStack()
        self.cnt = {}
        self.sems = {}
        self.mark = {}

    def sem(self, name):
        self.sems[name] = self.es.enter_context(self.nc.semaphore(name))
        self.cnt[name] = 0

    def inc(self, instr, name, amt=1):
        instr.then_inc(self.sems[name], amt)
        self.cnt[name] += amt
        return self.cnt[name]

    def w(self, eng, name, val=None):
        v = self.cnt[name] if val is None else val
        if v > 0:
            eng.wait_ge(self.sems[name], v)


def build_program():
    p = Prog()
    nc = p.nc
    es = p.es
    io = {}
    io["x"] = nc.dram_tensor("x", [T, D], F32, kind="ExternalInput").ap()
    io["wblob"] = nc.dram_tensor("wblob", [NE, WBLOB_PER], F32, kind="ExternalInput").ap()
    io["gwT"] = nc.dram_tensor("gwT", [D, 16], F32, kind="ExternalInput").ap()
    io["wq"] = nc.dram_tensor("wq", [D, D], F32, kind="ExternalInput").ap()
    io["wkv"] = nc.dram_tensor("wkv", [D, 128], F32, kind="ExternalInput").ap()
    io["wk"] = nc.dram_tensor("wk", [128, 512], F32, kind="ExternalInput").ap()
    io["wv"] = nc.dram_tensor("wv", [128, 512], F32, kind="ExternalInput").ap()
    io["wo"] = nc.dram_tensor("wo", [512, D], F32, kind="ExternalInput").ap()
    io["cf"] = nc.dram_tensor("cf", [128, CF_COLS], F32, kind="ExternalInput").ap()
    io["cb"] = nc.dram_tensor("cb", [128, CB_COLS], F32, kind="ExternalInput").ap()
    io["cbm"] = nc.dram_tensor("cbm", [128, 128], F32, kind="ExternalInput").ap()
    io["out"] = nc.dram_tensor("out", [T, D], F32, kind="ExternalOutput").ap()

    sb = {}
    sb["x_tm"] = es.enter_context(nc.sbuf_tensor("x_tm", [128, NT * D], F32))
    sb["ring"] = es.enter_context(nc.sbuf_tensor("ring", [128, 2 * SLOT_COLS], BF16))
    sb["gatew"] = es.enter_context(nc.sbuf_tensor("gatew", [128, 8 * 16], BF16))
    sb["cf"] = es.enter_context(nc.sbuf_tensor("cfsb", [128, CF_COLS], F32))
    sb["cb"] = es.enter_context(nc.sbuf_tensor("cbsb", [128, CB_COLS], BF16))
    sb["cpool"] = es.enter_context(nc.sbuf_tensor("cpool", [128, len(SCALAR_CONSTS)], F32))
    sb["misc"] = es.enter_context(nc.sbuf_tensor("miscsb", [128, 32], F32))
    sb["lnst"] = es.enter_context(nc.sbuf_tensor("lnst", [128, 192], F32))

    for s in ("DX", "DW", "T", "V", "S", "WF", "O"):
        p.sem(s)

    st = Ctx(p, io, sb)
    st.block = es.enter_context(nc.Block())
    emit_init(st)
    emit_ffn(st, 0)
    emit_attention(st)
    emit_ffn(st, 1)
    emit_out(st)
    es.close()
    return nc


class Ctx:
    def __init__(self, p, io, sb):
        self.p = p
        self.nc = p.nc
        self.io = io
        self.sb = sb
        self.m = p.mark


def emit_init(st):
    p, block, io, sb = st.p, st.block, st.io, st.sb

    @block.sync
    def _(e):
        for g in range(4):
            src = io["x"].rearrange("(n p) d -> p n d", p=128)[:, 4 * g:4 * (g + 1), :]
            dst = sb["x_tm"][:, 4 * g * D:(4 * g + 4) * D].rearrange("p (n d) -> p n d", d=D)
            p.inc(e.dma_start(out=dst, in_=src), "DX", 16)
        p.inc(e.dma_start(out=sb["cf"][:], in_=io["cf"]), "DX", 16)
    st.m["xin_DX"] = p.cnt["DX"]

    @block.gpsimd
    def _(e):
        p.inc(e.dma_start(out=sb["cb"][:], in_=io["cb"]), "DW", 16)
        p.inc(e.dma_start(out=sb["gatew"][:].rearrange("p (c n) -> p c n", n=16),
                          in_=io["gwT"].rearrange("(c p) n -> p c n", p=128)), "DW", 16)
    st.m["init_DW"] = p.cnt["DW"]

    @block.vector
    def _(e):
        for i, v in enumerate(SCALAR_CONSTS):
            e.memset(sb["cpool"][:, i:i + 1], v)
        p.w(e, "DX")
        ins = e.tensor_scalar(sb["misc"][:, MI_QG8:MI_QG8 + 1],
                              sb["cf"][:, CF_QG:CF_QG + 1], 0.125, None, ALU.mult)
        e.drain()
        p.inc(ins, "V")
    st.m["consts_V"] = p.cnt["V"]
    for i, v in enumerate(SCALAR_CONSTS):
        st.nc.const_aps.register(F32, v, sb["cpool"][:, i:i + 1])


def emit_ln(st, kind, xn_fm, dump, xn_tmp, ps_tp):
    p, block, sb = st.p, st.block, st.sb
    lnst = sb["lnst"]
    x_tm = sb["x_tm"]

    @block.scalar
    def _(e):
        p.w(e, "DX", st.m["xin_DX"])
        if st.m.get("x_ready_V"):
            p.w(e, "V", st.m["x_ready_V"])
        for tt in range(NT):
            xt = x_tm[:, tt * D:(tt + 1) * D]
            e.activation(dump[:], xt[:, 0:512], AF.Copy,
                         accum_out=lnst[:, LN_SUMA + tt:LN_SUMA + tt + 1])
            e.activation(dump[:], xt[:, 512:1024], AF.Copy,
                         accum_out=lnst[:, LN_SUMB + tt:LN_SUMB + tt + 1])
            e.activation(dump[:], xt[:, 0:512], AF.Square,
                         accum_out=lnst[:, LN_SQA + tt:LN_SQA + tt + 1])
            ins = e.activation(dump[:], xt[:, 512:1024], AF.Square,
                               accum_out=lnst[:, LN_SQB + tt:LN_SQB + tt + 1])
        p.inc(ins, "S")
    stats_S = p.cnt["S"]

    @block.vector
    def _(e):
        p.w(e, "S", stats_S)
        mu = lnst[:, LN_MU:LN_MU + 16]
        var = lnst[:, LN_VAR:LN_VAR + 16]
        t1 = lnst[:, LN_T1:LN_T1 + 16]
        e.tensor_tensor(mu, lnst[:, LN_SUMA:LN_SUMA + 16],
                        lnst[:, LN_SUMB:LN_SUMB + 16], ALU.add)
        e.drain()
        e.tensor_scalar(mu, mu, 1.0 / D, None, ALU.mult)
        e.drain()
        e.tensor_tensor(var, lnst[:, LN_SQA:LN_SQA + 16],
                        lnst[:, LN_SQB:LN_SQB + 16], ALU.add)
        e.drain()
        e.tensor_scalar(var, var, 1.0 / D, None, ALU.mult)
        e.drain()
        e.tensor_tensor(t1, mu, mu, ALU.mult)
        e.drain()
        ins = e.tensor_tensor(var, var, t1, ALU.subtract)
        e.drain()
        p.inc(ins, "V")
    var_V = p.cnt["V"]

    @block.scalar
    def _(e):
        p.w(e, "V", var_V)
        e.activation(lnst[:, LN_T2:LN_T2 + 16], lnst[:, LN_VAR:LN_VAR + 16],
                     AF.Ln, bias=LN_EPS)
        e.drain()
        ins = e.activation(lnst[:, LN_INV:LN_INV + 16], lnst[:, LN_T2:LN_T2 + 16],
                           AF.Exp, scale=-0.5)
        e.drain()
        p.inc(ins, "S")
    inv_S = p.cnt["S"]

    @block.vector
    def _(e):
        p.w(e, "S", inv_S)
        inv = lnst[:, LN_INV:LN_INV + 16]
        mu = lnst[:, LN_MU:LN_MU + 16]
        e.scalar_tensor_tensor(lnst[:, LN_B:LN_B + 16], mu, -1.0, inv, ALU.mult, ALU.mult)
        e.drain()
        if kind == "ffn":
            e.tensor_scalar(lnst[:, LN_A2:LN_A2 + 16], inv, 0.5, 1.0, ALU.mult, ALU.add)
            e.tensor_scalar(lnst[:, LN_B2:LN_B2 + 16], lnst[:, LN_B:LN_B + 16],
                            0.5, None, ALU.mult)
            e.drain()
        p.inc(e.nop(), "V")
    coef_V = p.cnt["V"]

    evac_marks = {}
    grp_tp_last = {}
    grp_S = {}
    step = 0
    for g in range(8):
        @block.scalar
        def _(e, g=g):
            p.w(e, "V", coef_V)
            if g > 0:
                p.w(e, "T", grp_tp_last[g - 1])
            for ti in range(2):
                tt = 2 * g + ti
                xt = x_tm[:, tt * D:(tt + 1) * D]
                ins_last = e.activation(xn_tmp[:, ti * D:(ti + 1) * D], xt, AF.Identity,
                                        scale=lnst[:, LN_INV + tt:LN_INV + tt + 1],
                                        bias=lnst[:, LN_B + tt:LN_B + tt + 1])
                if kind == "ffn":
                    ins_last = e.activation(xt, xt, AF.Identity,
                                            scale=lnst[:, LN_A2 + tt:LN_A2 + tt + 1],
                                            bias=lnst[:, LN_B2 + tt:LN_B2 + tt + 1])
            p.inc(ins_last, "S")
        grp_S[g] = p.cnt["S"]

        for fc in range(FC):
            pp = step % 2

            @block.tensor
            def _(e, g=g, fc=fc, pp=pp, step=step):
                if step == 0:
                    p.w(e, "DW", st.m["init_DW"])
                p.w(e, "S", grp_S[g])
                if step >= 2:
                    p.w(e, "V", evac_marks[step - 2])
                for ti in range(2):
                    ins_last = e.transpose(ps_tp[pp][:, ti * 1024:ti * 1024 + 128],
                                xn_tmp[:, ti * D + fc * 128: ti * D + (fc + 1) * 128],
                                sb["cb"][:, CB_ID:CB_ID + 128])
                p.inc(ins_last, "T")
            tp_T = p.cnt["T"]

            @block.vector
            def _(e, g=g, fc=fc, pp=pp, tp_T=tp_T):
                p.w(e, "T", tp_T)
                ins = e.tensor_copy(
                    xn_fm[:, fc * T + g * 256: fc * T + (g + 1) * 256]
                    .rearrange("p (b n) -> p b n", n=128),
                    ps_tp[pp][:].rearrange("p (b n) -> p b n", n=1024)[:, :, 0:128])
                p.inc(ins, "V")
            evac_marks[step] = p.cnt["V"]
            step += 1
        grp_tp_last[g] = p.cnt["T"]

    st.m["xn_T"] = p.cnt["T"]
    st.m["xn_V"] = p.cnt["V"]


def emit_ffn(st, call):
    p, nc = st.p, st.nc
    with (
        nc.sbuf_tensor(f"xnf{call}", [128, FC * T], BF16) as xn_fm,
        nc.sbuf_tensor(f"cwh{call}", [128, 256], F32) as cwh,
    ):
        with (
            nc.sbuf_tensor(f"lsc{call}", [128, 512], F32) as ln_dump,
            nc.sbuf_tensor(f"xnt{call}", [128, 2 * D], BF16) as xn_tmp,
            nc.psum_tensor(f"ptpa{call}", [128, 2048], BF16) as tp_a,
            nc.psum_tensor(f"ptpb{call}", [128, 2048], BF16) as tp_b,
        ):
            emit_ln(st, "ffn", xn_fm, ln_dump, xn_tmp, [tp_a, tp_b])
        with (
            nc.psum_tensor(f"prt{call}", [128, 512], F32) as ps_rt,
            nc.sbuf_tensor(f"rts{call}", [128, 256], F32) as rt_s,
            nc.sbuf_tensor(f"rtf{call}", [128, 256], F32) as rt_sfc,
            nc.sbuf_tensor(f"rtt{call}", [128, 384], F32) as rt_t,
            nc.sbuf_tensor(f"rtg{call}", [128, 64], F32) as rt_g,
            nc.sbuf_tensor(f"rtm{call}", [128, 128], F32) as rt_m8,
        ):
            emit_routing(st, xn_fm, ps_rt, rt_s, rt_sfc, rt_t, rt_g, rt_m8, cwh)
        with (
            nc.sbuf_tensor(f"act{call}", [128, IC * T], BF16) as act_fm,
            nc.sbuf_tensor(f"sgt{call}", [128, 2 * 512], F32) as sg_tmp,
            nc.psum_tensor(f"pga{call}", [128, 512], F32) as g_a,
            nc.psum_tensor(f"pgb{call}", [128, 512], F32) as g_b,
            nc.psum_tensor(f"pua{call}", [128, 512], F32) as u_a,
            nc.psum_tensor(f"pub{call}", [128, 512], F32) as u_b,
            nc.psum_tensor(f"pda{call}", [128, 512], F32) as d_a,
            nc.psum_tensor(f"pdb{call}", [128, 512], F32) as d_b,
        ):
            bufs = dict(xn=xn_fm, act=act_fm, sg=sg_tmp, cwh=cwh,
                        ps_g=[g_a, g_b], ps_u=[u_a, u_b], ps_d=[d_a, d_b])
            for ei in range(NE):
                emit_expert(st, call, ei, bufs)
    st.m["x_ready_V"] = p.cnt["V"]


def emit_routing(st, xn_fm, ps_rt, rt_s, rt_sfc, rt_t, rt_g, rt_m8, cwh):
    p, block, sb = st.p, st.block, st.sb

    @block.tensor
    def _(e):
        p.w(e, "DW", st.m["init_DW"])
        for tt in range(NT):
            for fc in range(FC):
                ins_last = e.matmul(ps_rt[:, tt * 16:(tt + 1) * 16],
                         xn_fm[:, fc * T + tt * 128: fc * T + (tt + 1) * 128],
                         sb["gatew"][:, fc * 16:(fc + 1) * 16],
                         start=(fc == 0), stop=(fc == FC - 1))
        p.inc(ins_last, "T")
    rt_T = p.cnt["T"]

    @block.scalar
    def _(e):
        p.w(e, "T", rt_T)
        p.inc(e.activation(rt_s[:], ps_rt[:, 0:256], AF.Sigmoid), "S")
    sig_S = p.cnt["S"]
    st.m["rt_sig_S"] = sig_S

    @block.vector
    def _(e):
        p.w(e, "S", sig_S)
        e.drain()
        gb = sb["cf"][:, CF_GB:CF_GB + 16].unsqueeze(1).to_broadcast(
            [128, 16, 16])
        e.tensor_tensor(rt_sfc[:].rearrange("p (t e) -> p t e", e=16),
                        rt_s[:].rearrange("p (t e) -> p t e", e=16), gb, ALU.add)
        e.drain()
        v4 = rt_sfc[:].rearrange("p (q e) -> p q e", e=4)
        for i, (a, b) in enumerate([(0, 1), (0, 2), (0, 3), (1, 2), (1, 3), (2, 3)]):
            e.tensor_tensor(rt_t[:, i * 64:(i + 1) * 64], v4[:, :, a], v4[:, :, b], ALU.add)
        e.drain()
        e.tensor_tensor(rt_g[:], rt_t[:, 0:64], rt_t[:, 64:128], ALU.max)
        e.drain()
        for i in range(2, 6):
            e.tensor_tensor(rt_g[:], rt_g[:], rt_t[:, i * 64:(i + 1) * 64], ALU.max)
        e.drain()
        gv = rt_g[:].rearrange("p (t g) -> p t g", g=4)
        t0, t1, t2, t3 = (rt_t[:, i * 16:(i + 1) * 16] for i in range(4))
        thr = rt_t[:, 64:80]
        e.tensor_tensor(t0, gv[:, :, 0], gv[:, :, 1], ALU.max)
        e.drain()
        e.tensor_tensor(t1, gv[:, :, 2], gv[:, :, 3], ALU.max)
        e.drain()
        e.tensor_tensor(t2, gv[:, :, 0], gv[:, :, 1], ALU.min)
        e.drain()
        e.tensor_tensor(t3, gv[:, :, 2], gv[:, :, 3], ALU.min)
        e.drain()
        e.tensor_tensor(thr, t0, t1, ALU.min)
        e.drain()
        e.tensor_tensor(thr, thr, t2, ALU.max)
        e.drain()
        e.tensor_tensor(thr, thr, t3, ALU.max)
        e.drain()
        pen = rt_t[:, 80:144]
        thrb = thr.unsqueeze(2).to_broadcast([128, 16, 4])
        e.tensor_tensor(pen.rearrange("p (t g) -> p t g", g=4), gv, thrb, ALU.is_lt)
        e.drain()
        e.tensor_scalar(pen, pen, -BIG, None, ALU.mult)
        e.drain()
        penb = pen.rearrange("p (t g) -> p t g", g=4).unsqueeze(3).to_broadcast([128, 16, 4, 4])
        sfcv = rt_sfc[:].rearrange("p (t g e) -> p t g e", g=4, e=4)
        e.tensor_tensor(sfcv, sfcv, penb, ALU.add)
        e.drain()
        for tt in range(NT):
            e.max(out=rt_m8[:, tt * 8:(tt + 1) * 8], in_=rt_sfc[:, tt * 16:(tt + 1) * 16])
        e.drain()
        thr4b = rt_m8[:].rearrange("p (t e) -> p t e", e=8)[:, :, 3:4].to_broadcast(
            [128, 16, 16])
        sel = rt_t[:, 128:384]
        e.tensor_tensor(sel.rearrange("p (t e) -> p t e", e=16),
                        rt_sfc[:].rearrange("p (t e) -> p t e", e=16), thr4b, ALU.is_ge)
        e.drain()
        e.tensor_tensor(sel, sel, rt_s[:], ALU.mult)
        sv = sel.rearrange("p (t e) -> p t e", e=16)
        e.drain()
        e.tensor_tensor(rt_t[:, 0:128].rearrange("p (t e) -> p t e", e=8),
                        sv[:, :, 0:8], sv[:, :, 8:16], ALU.add)
        e.drain()
        u1 = rt_t[:, 0:128].rearrange("p (t e) -> p t e", e=8)
        e.tensor_tensor(rt_g[:, 0:64].rearrange("p (t e) -> p t e", e=4),
                        u1[:, :, 0:4], u1[:, :, 4:8], ALU.add)
        e.drain()
        u2 = rt_g[:, 0:64].rearrange("p (t e) -> p t e", e=4)
        e.tensor_tensor(rt_t[:, 0:32].rearrange("p (t e) -> p t e", e=2),
                        u2[:, :, 0:2], u2[:, :, 2:4], ALU.add)
        e.drain()
        u3 = rt_t[:, 0:32].rearrange("p (t e) -> p t e", e=2)
        ins = e.tensor_tensor(rt_g[:, 0:16].unsqueeze(2),
                              u3[:, :, 0:1], u3[:, :, 1:2], ALU.add)
        e.drain()
        p.inc(ins, "V")
    wsum_V = p.cnt["V"]

    @block.scalar
    def _(e):
        p.w(e, "V", wsum_V)
        e.activation(rt_g[:, 16:32], rt_g[:, 0:16], AF.Ln, bias=1e-20)
        e.drain()
        ins = e.activation(rt_g[:, 32:48], rt_g[:, 16:32], AF.Exp,
                           scale=-1.0, bias=math.log(1.25))
        e.drain()
        p.inc(ins, "S")
    rf_S = p.cnt["S"]

    @block.vector
    def _(e):
        p.w(e, "S", rf_S)
        rfb = rt_g[:, 32:48].unsqueeze(2).to_broadcast([128, 16, 16])
        sel = rt_t[:, 128:384]
        ins = e.tensor_tensor(cwh[:].rearrange("p (t e) -> p t e", e=16),
                              sel.rearrange("p (t e) -> p t e", e=16), rfb, ALU.mult)
        p.inc(ins, "V")
    st.m["cwh_V"] = p.cnt["V"]


def emit_expert(st, call, ei, bufs):
    p, block, sb = st.p, st.block, st.sb
    slot = ei % 2
    base = slot * SLOT_COLS
    ring = sb["ring"]
    wg = ring[:, base:base + 4096]
    wu = ring[:, base + 4096:base + 8192]
    wd = ring[:, base + 8192:base + 12288]
    xn, act, sg, cwh = bufs["xn"], bufs["act"], bufs["sg"], bufs["cwh"]
    ps_g, ps_u, ps_d = bufs["ps_g"], bufs["ps_u"], bufs["ps_d"]

    @block.gpsimd
    def _(e):
        pf = st.m.get(f"slotfree_{slot}")
        if pf is not None:
            p.w(e, "WF", pf)
        src = st.io["wblob"][ei]
        for half in range(2):
            sap = src[half * 262144:(half + 1) * 262144].rearrange(
                "(c p n) -> p c n", p=128, n=512)
            p.inc(e.dma_start(out=wg[:, half * 2048:(half + 1) * 2048]
                              .rearrange("p (c n) -> p c n", n=512), in_=sap), "DW", 16)
            sap = src[524288 + half * 262144: 524288 + (half + 1) * 262144].rearrange(
                "(c p n) -> p c n", p=128, n=512)
            p.inc(e.dma_start(out=wu[:, half * 2048:(half + 1) * 2048]
                              .rearrange("p (c n) -> p c n", n=512), in_=sap), "DW", 16)
            sap = src[1048576 + half * 262144: 1048576 + (half + 1) * 262144].rearrange(
                "(c p n) -> p c n", p=128, n=1024)
            p.inc(e.dma_start(out=wd[:, half * 2048:(half + 1) * 2048]
                              .rearrange("p (c n) -> p c n", n=1024), in_=sap), "DW", 16)
    w_DW = p.cnt["DW"]

    for n in range(NTS):
        for m in range(IC):
            pp = (n * IC + m) % 2

            @block.tensor
            def _(e, n=n, m=m, pp=pp):
                p.w(e, "DW", w_DW)
                if ei == 0 and n == 0 and m == 0:
                    p.w(e, "V", st.m["xn_V"])
                    p.w(e, "S", st.m["rt_sig_S"])
                g = st.m.get(f"up_evac_{pp}")
                if g is not None:
                    p.w(e, "V", g)
                for fc in range(FC):
                    e.matmul(ps_g[pp][:],
                             wg[:, fc * 512 + m * 128: fc * 512 + (m + 1) * 128],
                             xn[:, fc * T + n * 512: fc * T + (n + 1) * 512],
                             start=(fc == 0), stop=(fc == FC - 1))
                for fc in range(FC):
                    ins_last = e.matmul(ps_u[pp][:],
                             wu[:, fc * 512 + m * 128: fc * 512 + (m + 1) * 128],
                             xn[:, fc * T + n * 512: fc * T + (n + 1) * 512],
                             start=(fc == 0), stop=(fc == FC - 1))
                p.inc(ins_last, "T")
            up_T = p.cnt["T"]

            @block.scalar
            def _(e, pp=pp, up_T=up_T):
                p.w(e, "T", up_T)
                g = st.m.get(f"up_mul_{pp}")
                if g is not None:
                    p.w(e, "V", g)
                ins = e.activation(sg[:, pp * 512:(pp + 1) * 512], ps_g[pp][:], AF.Silu)
                p.inc(ins, "S")
            silu_S = p.cnt["S"]

            @block.vector
            def _(e, n=n, m=m, pp=pp, silu_S=silu_S):
                p.w(e, "S", silu_S)
                ins = e.tensor_tensor(act[:, m * T + n * 512: m * T + (n + 1) * 512],
                                      sg[:, pp * 512:(pp + 1) * 512], ps_u[pp][:],
                                      ALU.mult)
                p.inc(ins, "V")
            st.m[f"up_evac_{pp}"] = p.cnt["V"]
            st.m[f"up_mul_{pp}"] = p.cnt["V"]
    act_V = p.cnt["V"]

    for tt in range(NT):
        for dc in range(2):
            pp = (tt * 2 + dc) % 2

            @block.tensor
            def _(e, tt=tt, dc=dc, pp=pp):
                if tt == 0 and dc == 0:
                    p.w(e, "V", act_V)
                g = st.m.get(f"dn_evac_{pp}")
                if g is not None:
                    p.w(e, "V", g)
                for ic in range(IC):
                    ins_last = e.matmul(ps_d[pp][:],
                             act[:, ic * T + tt * 128: ic * T + (tt + 1) * 128],
                             wd[:, ic * 1024 + dc * 512: ic * 1024 + (dc + 1) * 512],
                             start=(ic == 0), stop=(ic == IC - 1))
                p.inc(ins_last, "T")
            dn_T = p.cnt["T"]

            @block.vector
            def _(e, tt=tt, dc=dc, pp=pp, dn_T=dn_T):
                p.w(e, "T", dn_T)
                if ei < 16:
                    if tt == 0 and dc == 0:
                        p.w(e, "V", st.m["cwh_V"])
                    sc = cwh[:, tt * 16 + ei: tt * 16 + ei + 1]
                else:
                    sc = 0.5
                xd = sb["x_tm"][:, tt * D + dc * 512: tt * D + (dc + 1) * 512]
                ins = e.scalar_tensor_tensor(xd, ps_d[pp][:], sc, xd, ALU.mult, ALU.add)
                p.inc(ins, "V")
            st.m[f"dn_evac_{pp}"] = p.cnt["V"]

    @block.tensor
    def _(e):
        e.drain()
        p.inc(e.nop(), "WF")
    st.m[f"slotfree_{slot}"] = p.cnt["WF"]


def emit_attention(st):
    p, nc, sb = st.p, st.nc, st.sb
    ring = sb["ring"]
    wo = ring[:, 0:4096]
    wkv = ring[:, 4096:5120]
    wk = ring[:, 5120:5632]
    wv = ring[:, 5632:6144]
    wq = ring[:, 6144:14336]

    @st.block.gpsimd
    def _(e):
        p.w(e, "WF")
        p.inc(e.dma_start(out=wo.rearrange("p (c n) -> p c n", n=1024),
                          in_=st.io["wo"].rearrange("(c p) n -> p c n", p=128)), "DW", 16)
        p.inc(e.dma_start(out=wkv.rearrange("p (c n) -> p c n", n=128),
                          in_=st.io["wkv"].rearrange("(c p) n -> p c n", p=128)), "DW", 16)
        p.inc(e.dma_start(out=wk, in_=st.io["wk"]), "DW", 16)
        p.inc(e.dma_start(out=wv, in_=st.io["wv"]), "DW", 16)
        for half in range(2):
            p.inc(e.dma_start(
                out=wq[:, half * 4096:(half + 1) * 4096].rearrange(
                    "p (c n) -> p c n", n=1024),
                in_=st.io["wq"].rearrange("(c p) n -> p c n", p=128)
                [:, 4 * half:4 * (half + 1), :]), "DW", 16)
    st.m["aw_DW"] = p.cnt["DW"]

    with (
        nc.sbuf_tensor("qn", [128, 4 * T], BF16) as qn,
        nc.sbuf_tensor("kn", [128, 4 * T], BF16) as kn,
        nc.sbuf_tensor("vaug", [128, NT * 528], BF16) as vaug,
        nc.sbuf_tensor("sqt", [128, 512], F32) as sq_tmp,
        nc.sbuf_tensor("irms", [128, 512], F32) as irms,
    ):
        with (
            nc.sbuf_tensor("xnfa", [128, FC * T], BF16) as xn_fm,
        ):
            with (
                nc.sbuf_tensor("xnta", [128, 2 * D], BF16) as xn_tmp,
                nc.psum_tensor("ptpaa", [128, 2048], BF16) as tp_a,
                nc.psum_tensor("ptpab", [128, 2048], BF16) as tp_b,
            ):
                emit_ln(st, "attn", xn_fm, sq_tmp, xn_tmp, [tp_a, tp_b])
            with (
                nc.sbuf_tensor("kvsb", [128, T], BF16) as kv_sb,
                nc.psum_tensor("pqa", [128, 512], F32) as ps_qa,
                nc.psum_tensor("pqb", [128, 512], F32) as ps_qb,
                nc.psum_tensor("pss", [128, 512], F32) as ps_ssq,
                nc.psum_tensor("prb", [128, 512], F32) as ps_rb,
            ):
                emit_attn_prep(st, dict(qn=qn, kn=kn, vaug=vaug, sq=sq_tmp, irms=irms,
                                        xn=xn_fm, kv=kv_sb,
                                        ps_q=[ps_qa, ps_qb], ps_ssq=ps_ssq, ps_rb=ps_rb,
                                        wq=wq, wkv=wkv, wk=wk, wv=wv))
        with (
            nc.sbuf_tensor("probs", [128, 8 * 512], BF16) as probs,
            nc.sbuf_tensor("outctm", [128, 4 * 512], BF16) as outc_tm,
            nc.sbuf_tensor("outcfm", [128, 512], BF16) as outc_fm,
            nc.sbuf_tensor("densb", [128, 16], F32) as den_sb,
            nc.sbuf_tensor("masks", [128, 128], BF16) as masks,
        ):
            @st.block.gpsimd
            def _(e):
                p.w(e, "T")
                p.inc(e.dma_start(out=masks[:], in_=st.io["cbm"]), "DW", 16)
            st.m["mask_DW"] = p.cnt["DW"]
            emit_attn_scores(st, dict(qn=qn, kn=kn, vaug=vaug, probs=probs,
                                      outc_tm=outc_tm, outc_fm=outc_fm,
                                      den=den_sb, masks=masks, wo=wo))

    @st.block.tensor
    def _(e):
        e.drain()
        p.inc(e.nop(), "WF")
    st.m["slotfree_0"] = p.cnt["WF"]
    st.m["slotfree_1"] = p.cnt["WF"]
    st.m["x_ready_V"] = p.cnt["V"]


def emit_attn_prep(st, b):
    p, block, sb = st.p, st.block, st.sb
    ps_q, ps_ssq, ps_rb = b["ps_q"], b["ps_ssq"], b["ps_rb"]
    aw = st.m["aw_DW"]

    for ts in range(NTS):
        t0 = p.cnt["T"]
        s0 = p.cnt["S"]
        order = []
        for fco in range(FC):
            order.append(("q", fco))
            if fco >= 1:
                order.append(("ssq", fco - 1))
        order.append(("ssq", FC - 1))
        t_of = {ent: t0 + i + 1 for i, ent in enumerate(order)}
        s_of_sq = {}
        s_after = {}
        sc = s0
        for fco in range(FC):
            sc += 1
            s_of_sq[fco] = sc
            if fco < 4:
                sc += 1
            s_after[fco] = sc
        rms_prev_V = st.m.get("rms_t0_V")

        @block.tensor
        def _(e, ts=ts, order=order, t_of=t_of, s_of_sq=s_of_sq, s_after=s_after,
              rms_prev_V=rms_prev_V, s0=s0):
            for ent in order:
                kind, fco = ent
                if kind == "q":
                    pp = fco % 2
                    if ts == 0 and fco == 0:
                        p.w(e, "DW", aw)
                        p.w(e, "V", st.m["xn_V"])
                    if fco >= 2:
                        p.w(e, "S", s_after[fco - 2])
                    elif ts > 0 and fco == 0:
                        p.w(e, "S", s0)
                    for fc in range(FC):
                        ins_last = e.matmul(
                            ps_q[pp][:],
                            b["wq"][:, fc * 1024 + fco * 128: fc * 1024 + (fco + 1) * 128],
                            b["xn"][:, fc * T + ts * 512: fc * T + (ts + 1) * 512],
                            start=(fc == 0), stop=(fc == FC - 1))
                    p.inc(ins_last, "T")
                else:
                    if fco == 0 and rms_prev_V is not None:
                        p.w(e, "V", rms_prev_V)
                    p.w(e, "S", s_of_sq[fco])
                    ins_last = e.matmul(
                        ps_ssq[0:16, 0:512],
                        sb["cf"][:, CF_INDQ + fco * 16: CF_INDQ + (fco + 1) * 16],
                        b["sq"][:], start=(fco == 0), stop=(fco == FC - 1))
                    p.inc(ins_last, "T")

        @block.scalar
        def _(e, ts=ts, t_of=t_of):
            for fco in range(FC):
                pp = fco % 2
                p.w(e, "T", t_of[("q", fco)])
                ins = e.activation(b["sq"][:], ps_q[pp][:], AF.Square)
                p.inc(ins, "S")
                if fco < 4:
                    ins = e.activation(
                        b["qn"][:, fco * T + ts * 512: fco * T + (ts + 1) * 512],
                        ps_q[pp][:], AF.Copy)
                    p.inc(ins, "S")
        assert p.cnt["T"] == t0 + len(order)
        assert p.cnt["S"] == s_after[FC - 1]
        ssq_T = p.cnt["T"]
        emit_rms_apply(st, b, ts, ssq_T, 64, b["qn"], b["qn"],
                       sb["misc"][:, MI_QG8:MI_QG8 + 1])
    st.m["qn_V"] = p.cnt["V"]
    qph_S = p.cnt["S"]

    t0k = p.cnt["T"]
    s0k = p.cnt["S"]

    @block.tensor
    def _(e, t0k=t0k, s0k=s0k, qph_S=qph_S):
        for ts in range(NTS):
            pp = ts % 2
            if ts == 0:
                p.w(e, "S", qph_S)
            if ts >= 2:
                p.w(e, "S", s0k + ts - 1)
            for fc in range(FC):
                ins_last = e.matmul(
                    ps_q[pp][:], b["wkv"][:, fc * 128:(fc + 1) * 128],
                    b["xn"][:, fc * T + ts * 512: fc * T + (ts + 1) * 512],
                    start=(fc == 0), stop=(fc == FC - 1))
            p.inc(ins_last, "T")

    @block.scalar
    def _(e, t0k=t0k):
        for ts in range(NTS):
            pp = ts % 2
            p.w(e, "T", t0k + ts + 1)
            ins = e.activation(b["kv"][:, ts * 512:(ts + 1) * 512], ps_q[pp][:], AF.Copy)
            p.inc(ins, "S")
    kv_S = p.cnt["S"]
    assert kv_S == s0k + 4

    for ts in range(NTS):
        t0 = p.cnt["T"]
        s0 = p.cnt["S"]
        order = [("k", 0), ("k", 1), ("ssq", 0), ("k", 2), ("ssq", 1), ("k", 3),
                 ("ssq", 2), ("ssq", 3)]
        t_of = {ent: t0 + i + 1 for i, ent in enumerate(order)}
        s_of_sq = {kc: s0 + 2 * kc + 1 for kc in range(4)}
        s_after = {kc: s0 + 2 * kc + 2 for kc in range(4)}
        rms_prev_V = st.m.get("rms_t0_V")

        @block.tensor
        def _(e, ts=ts, order=order, t_of=t_of, s_of_sq=s_of_sq, s_after=s_after,
              rms_prev_V=rms_prev_V, s0=s0, kv_S=kv_S):
            for ent in order:
                kind, kc = ent
                if kind == "k":
                    pp = kc % 2
                    if ts == 0 and kc == 0:
                        p.w(e, "S", kv_S)
                    if kc >= 2:
                        p.w(e, "S", s_after[kc - 2])
                    elif ts > 0 and kc == 0:
                        p.w(e, "S", s0)
                    ins_last = e.matmul(
                        ps_q[pp][:], b["wk"][:, kc * 128:(kc + 1) * 128],
                        b["kv"][:, ts * 512:(ts + 1) * 512], start=True, stop=True)
                    p.inc(ins_last, "T")
                else:
                    if kc == 0 and rms_prev_V is not None:
                        p.w(e, "V", rms_prev_V)
                    p.w(e, "S", s_of_sq[kc])
                    ins_last = e.matmul(
                        ps_ssq[0:16, 0:512],
                        sb["cf"][:, CF_INDK + kc * 16: CF_INDK + (kc + 1) * 16],
                        b["sq"][:], start=(kc == 0), stop=(kc == 3))
                    p.inc(ins_last, "T")

        @block.scalar
        def _(e, ts=ts, t_of=t_of):
            for kc in range(4):
                pp = kc % 2
                p.w(e, "T", t_of[("k", kc)])
                ins = e.activation(b["sq"][:], ps_q[pp][:], AF.Square)
                p.inc(ins, "S")
                ins = e.activation(
                    b["kn"][:, kc * T + ts * 512: kc * T + (ts + 1) * 512],
                    ps_q[pp][:], AF.Copy)
                p.inc(ins, "S")
        ssq_T = p.cnt["T"]
        emit_rms_apply(st, b, ts, ssq_T, 32, b["kn"], b["kn"],
                       sb["cf"][:, CF_KG:CF_KG + 1])
    st.m["kn_V"] = p.cnt["V"]
    kph_S = p.cnt["S"]

    v0v = p.cnt["V"]
    t0v = p.cnt["T"]

    @block.tensor
    def _(e, v0v=v0v, kph_S=kph_S):
        for tt in range(NT):
            pp = tt % 2
            if tt == 0:
                p.w(e, "S", kph_S)
                p.w(e, "V", st.m["kn_V"])
            if tt >= 2:
                p.w(e, "V", v0v + tt)
            ins_last = e.matmul(ps_q[pp][:], b["kv"][:, tt * 128:(tt + 1) * 128],
                                b["wv"], start=True, stop=True)
            p.inc(ins_last, "T")

    @block.vector
    def _(e, t0v=t0v):
        for tt in range(NT):
            ones = b["vaug"][:, tt * 528:(tt + 1) * 528].rearrange(
                "p (h c) -> p h c", c=33)[:, :, 32:33]
            ins = e.memset(ones, 1.0)
            if tt == 0:
                p.inc(ins, "V")
        for tt in range(NT):
            pp = tt % 2
            p.w(e, "T", t0v + tt + 1)
            dst = b["vaug"][:, tt * 528:(tt + 1) * 528].rearrange(
                "p (h c) -> p h c", c=33)[:, :, 0:32]
            src = ps_q[pp][:].rearrange("p (h c) -> p h c", c=32)
            ins = e.tensor_copy(dst, src)
            p.inc(ins, "V")
    assert p.cnt["V"] == v0v + 17

    emit_M(st, b)


def emit_rms_apply(st, b, ts, ssq_T, nd, raw, out, gcol):
    p, block, sb = st.p, st.block, st.sb
    ps_ssq, ps_rb = b["ps_ssq"], b["ps_rb"]

    @block.vector
    def _(e):
        p.w(e, "T", ssq_T)
        ins = e.tensor_scalar(b["irms"][0:16, 0:512], ps_ssq[0:16, 0:512],
                              1.0 / nd, RMS_EPS, ALU.mult, ALU.add)
        p.inc(ins, "V")
    t0_V = p.cnt["V"]
    st.m["rms_t0_V"] = t0_V

    @block.scalar
    def _(e):
        p.w(e, "V", t0_V)
        e.activation(b["irms"][0:16, 0:512], b["irms"][0:16, 0:512], AF.Ln)
        ins = e.activation(b["irms"][0:16, 0:512], b["irms"][0:16, 0:512],
                           AF.Exp, scale=-0.5)
        p.inc(ins, "S")
    inv_S = p.cnt["S"]

    for qc in range(4):
        @block.tensor
        def _(e, qc=qc):
            p.w(e, "S", inv_S)
            g = st.m.get("rb_used_V")
            if g is not None:
                p.w(e, "V", g)
            ins_last = e.matmul(ps_rb[:], sb["cf"][0:16, CF_SEL + qc * 128: CF_SEL + (qc + 1) * 128],
                     b["irms"][0:16, 0:512], start=True, stop=True)
            p.inc(ins_last, "T")
        rb_T = p.cnt["T"]

        @block.vector
        def _(e, qc=qc, rb_T=rb_T):
            p.w(e, "T", rb_T)
            ins = e.scalar_tensor_tensor(
                out[:, qc * T + ts * 512: qc * T + (ts + 1) * 512],
                ps_rb[:], gcol, raw[:, qc * T + ts * 512: qc * T + (ts + 1) * 512],
                ALU.mult, ALU.mult)
            p.inc(ins, "V")
        st.m["rb_used_V"] = p.cnt["V"]


def emit_M(st, b):
    p, block, sb = st.p, st.block, st.sb
    ps_ssq, ps_rb = b["ps_ssq"], b["ps_rb"]
    for which, buf, mcol in (("q", b["qn"], MI_MQ), ("k", b["kn"], MI_MK)):
        for ts in range(NTS):
            for qc in range(4):
                @block.vector
                def _(e, ts=ts, qc=qc, buf=buf, which=which):
                    g = st.m.get("msq_T")
                    if g is not None:
                        p.w(e, "T", g)
                    if which == "q" and ts == 0 and qc == 0:
                        p.w(e, "T")
                        p.w(e, "V", st.m["qn_V"])
                    if which == "k" and ts == 0 and qc == 0:
                        p.w(e, "V", st.m["kn_V"])
                    src = buf[:, qc * T + ts * 512: qc * T + (ts + 1) * 512]
                    ins = e.tensor_tensor(b["sq"][:], src, src, ALU.mult)
                    p.inc(ins, "V")
                msq_V = p.cnt["V"]

                @block.tensor
                def _(e, qc=qc, msq_V=msq_V):
                    p.w(e, "V", msq_V)
                    g = st.m.get("mred_V")
                    if g is not None and qc == 0:
                        p.w(e, "V", g)
                    ins_last = e.matmul(ps_ssq[0:16, 0:512],
                             sb["cf"][:, CF_INDK + qc * 16: CF_INDK + (qc + 1) * 16],
                             b["sq"][:], start=(qc == 0), stop=(qc == 3))
                    p.inc(ins_last, "T")
                st.m["msq_T"] = p.cnt["T"]
            ssq_T = p.cnt["T"]

            @block.vector
            def _(e, ts=ts, ssq_T=ssq_T, mcol=mcol):
                p.w(e, "T", ssq_T)
                e.reduce_max(b["irms"][0:16, 0:1], ps_ssq[0:16, 0:512], axis=AX.X)
                e.drain()
                if ts == 0:
                    ins = e.tensor_copy(sb["misc"][0:16, mcol:mcol + 1],
                                        b["irms"][0:16, 0:1])
                else:
                    ins = e.tensor_tensor(sb["misc"][0:16, mcol:mcol + 1],
                                          sb["misc"][0:16, mcol:mcol + 1],
                                          b["irms"][0:16, 0:1], ALU.max)
                e.drain()
                p.inc(ins, "V")
            st.m["mred_V"] = p.cnt["V"]

    @block.vector
    def _(e):
        ins = e.tensor_tensor(sb["misc"][0:16, MI_MH:MI_MH + 1],
                              sb["misc"][0:16, MI_MQ:MI_MQ + 1],
                              sb["misc"][0:16, MI_MK:MI_MK + 1], ALU.mult)
        e.drain()
        p.inc(ins, "V")
    mh_V = p.cnt["V"]

    @block.scalar
    def _(e):
        p.w(e, "V", mh_V)
        e.activation(sb["misc"][0:16, MI_MH:MI_MH + 1],
                     sb["misc"][0:16, MI_MH:MI_MH + 1], AF.Ln)
        e.drain()
        ins = e.activation(sb["misc"][0:16, MI_MH:MI_MH + 1],
                           sb["misc"][0:16, MI_MH:MI_MH + 1], AF.Exp,
                           scale=0.5, bias=0.02)
        e.drain()
        p.inc(ins, "S")
    mh_S = p.cnt["S"]

    @block.vector
    def _(e):
        p.w(e, "S", mh_S)
        ins = e.tensor_copy(b["irms"][0:16, 0:128],
                            sb["misc"][0:16, MI_MH:MI_MH + 1].to_broadcast([16, 128]))
        e.drain()
        p.inc(ins, "V")
    mb_V = p.cnt["V"]

    @block.tensor
    def _(e):
        p.w(e, "V", mb_V)
        g = st.m.get("rb_used_V")
        if g is not None:
            p.w(e, "V", g)
        ins_last = e.matmul(ps_rb[:, 0:16], b["irms"][0:16, 0:128],
                 sb["cf"][0:16, CF_I16:CF_I16 + 16], start=True, stop=True)
        p.inc(ins_last, "T")
    nm_T = p.cnt["T"]

    @block.vector
    def _(e):
        p.w(e, "T", nm_T)
        e.tensor_scalar(sb["misc"][:, MI_NEGM:MI_NEGM + 16], ps_rb[:, 0:16],
                        -1.0, None, ALU.mult)
        e.drain()
        ng = sb["misc"][:].rearrange("p c -> p c 1") if False else None
        for qc in range(4):
            c = MI_NEGM + 4 * qc
            e.tensor_tensor(sb["misc"][:, MI_NEGMC + qc:MI_NEGMC + qc + 1],
                            sb["misc"][:, c:c + 1], sb["misc"][:, c + 1:c + 2], ALU.min)
            e.drain()
            e.tensor_tensor(sb["misc"][:, MI_NEGMC + qc:MI_NEGMC + qc + 1],
                            sb["misc"][:, MI_NEGMC + qc:MI_NEGMC + qc + 1],
                            sb["misc"][:, c + 2:c + 3], ALU.min)
            e.drain()
            ins = e.tensor_tensor(sb["misc"][:, MI_NEGMC + qc:MI_NEGMC + qc + 1],
                                  sb["misc"][:, MI_NEGMC + qc:MI_NEGMC + qc + 1],
                                  sb["misc"][:, c + 3:c + 4], ALU.min)
            e.drain()
        p.inc(ins, "V")
    st.m["negM_V"] = p.cnt["V"]


def emit_attn_scores(st, b):
    p, block, nc, sb = st.p, st.block, st.nc, st.sb
    probs, outc_tm, outc_fm, den = b["probs"], b["outc_tm"], b["outc_fm"], b["den"]

    def pv_base(j):
        return (j // 2) * 512 + (j % 2) * 256

    for s in range(4):
        n_kc = 4 * (s + 1)
        n_g = 2 * n_kc
        with (
            nc.psum_tensor(f"psca_{s}", [128, 1024], F32) as scbA,
            nc.psum_tensor(f"pscb_{s}", [128, 1024], F32) as scbB,
            nc.psum_tensor(f"ppv_{s}", [128, 1024], F32) as pvb,
        ):
            scb = [scbA, scbB]
            for qc in range(4):
                t0 = p.cnt["T"]
                s0 = p.cnt["S"]
                v0 = p.cnt["V"]

                order = [("sc", 0)]
                for g in range(1, n_g):
                    order.append(("sc", g))
                    order.append(("pv", g - 1))
                order.append(("pv", n_g - 1))
                t_map = {ent: t0 + i + 1 for i, ent in enumerate(order)}

                def t_sc(g):
                    return t_map[("sc", g)]

                def t_pv(g):
                    return t_map[("pv", g)]

                def s_exp(g):
                    return s0 + g + 1

                tri_v = {}
                vc = v0 + 1
                for kc in range(4 * s, n_kc):
                    for h in range(2):
                        vc += 1
                        tri_v[kc * 2 + h] = vc

                @block.vector
                def _(e, s=s, qc=qc, n_kc=n_kc, t0=t0, v0=v0, t_map=t_map,
                      s0=s0):
                    g0 = st.m.get("strip_T_end")
                    if g0 is not None and qc == 0:
                        p.w(e, "T", g0)
                    ins = e.memset(pvb[:], 0.0)
                    p.inc(ins, "V")
                    for kc in range(4 * s, n_kc):
                        d = kc - 4 * s
                        for h in range(2):
                            g = kc * 2 + h
                            p.w(e, "S", s0 + g + 1)
                            if s == 0 and qc == 0 and kc == 4 * s and h == 0:
                                p.w(e, "DW", st.m["mask_DW"])
                            pslot = kc % 2
                            dst = probs[:, (4 * h + pslot) * 512 + d * 128:]
                            dst = bass.AP(dst.tensor, dst.offset,
                                          [[4096, 128], [1024, 2], [1, 128]]) \
                                if False else probs[:].rearrange(
                                    "p (sl c) -> p sl c", c=512)[
                                    :, 4 * h + pslot:4 * h + pslot + 3:2,
                                    d * 128:(d + 1) * 128]
                            tri = b["masks"][:, 0:128].unsqueeze(1).to_broadcast(
                                [128, 2, 128])
                            if K_DBG & 1:
                                ins = e.nop()
                            else:
                                ins = e.tensor_tensor(dst, dst, tri, ALU.mult)
                            if K_DBG & 4 and s == 0 and qc == 0 and g == 0:
                                e.drain()
                                e.tensor_copy(sb["x_tm"][:, 0:512],
                                              probs[:, 0:512])
                                e.tensor_copy(sb["x_tm"][:, 512:640],
                                              b["masks"][:, 0:128])
                            p.inc(ins, "V")

                @block.tensor
                def _(e, s=s, qc=qc, n_kc=n_kc, n_g=n_g, t0=t0, s0=s0, v0=v0,
                      order=order):
                    for ent in order:
                        kind, g = ent
                        kc, h = g // 2, g % 2
                        pslot = kc % 2
                        if kind == "sc":
                            if g == 0:
                                if s == 0 and qc == 0:
                                    p.w(e, "V", st.m["negM_V"])
                                if qc == 0:
                                    for dcx in range(2):
                                        gm = st.m.get(f"woev_{dcx}")
                                        if gm is not None:
                                            p.w(e, "V", gm)
                                p.w(e, "S", s0)
                            if g >= 2:
                                p.w(e, "S", s_exp(g - 2))
                            for jj in range(2):
                                j = 2 * h + jj
                                ins_last = e.matmul(
                                    scb[g % 2][:, jj * 512:(jj + 1) * 512],
                                    b["kn"][32 * j:32 * (j + 1),
                                            qc * T + kc * 128: qc * T + (kc + 1) * 128],
                                    b["qn"][32 * j:32 * (j + 1),
                                            qc * T + s * 512: qc * T + (s + 1) * 512],
                                    start=True, stop=True,
                                    tile_position=(32 * j, 0))
                            p.inc(ins_last, "T")
                        else:
                            d = kc - 4 * s
                            p.w(e, "S", s_exp(g))
                            if d >= 0:
                                p.w(e, "V", tri_v[g])
                            if g == 0:
                                p.w(e, "V", v0 + 1)
                            ins_last = None
                            for jj in range(2):
                                j = 2 * h + jj
                                for qt in range(4):
                                    if (not (K_DBG & 2)) and d >= 0 and d > qt:
                                        continue
                                    ins_last = e.matmul(
                                        pvb[:, pv_base(j) + qt * 33:
                                            pv_base(j) + qt * 33 + 33],
                                        probs[:, (j * 2 + pslot) * 512 + qt * 128:
                                              (j * 2 + pslot) * 512 + (qt + 1) * 128],
                                        b["vaug"][:, kc * 528 + (4 * qc + j) * 33:
                                                  kc * 528 + (4 * qc + j) * 33 + 33],
                                        start=False,
                                        stop=(kc == ((n_kc - 1) if (K_DBG & 2)
                                              else min(n_kc - 1, 4 * s + qt))))
                            p.inc(ins_last, "T")

                @block.scalar
                def _(e, s=s, qc=qc, n_g=n_g, t0=t0, v0=v0):
                    for g in range(n_g):
                        kc, h = g // 2, g % 2
                        pslot = kc % 2
                        if g < 4:
                            p.w(e, "T", t0)
                        else:
                            p.w(e, "T", t_pv(g - 4))
                        p.w(e, "T", t_sc(g))
                        dst = probs[:].rearrange("p (sl c) -> p sl c", c=512)[
                            :, 4 * h + pslot:4 * h + pslot + 3:2, :]
                        ins = e.activation(
                            dst, scb[g % 2][:], AF.Exp,
                            bias=sb["misc"][:, MI_NEGMC + qc:MI_NEGMC + qc + 1])
                        p.inc(ins, "S")

                assert p.cnt["T"] == t0 + 2 * n_g
                assert p.cnt["S"] == s0 + n_g
                assert p.cnt["V"] == v0 + 9
                pv_T = p.cnt["T"]

                @block.vector
                def _(e, qc=qc, pv_T=pv_T):
                    p.w(e, "T", pv_T)
                    for j in range(4):
                        ins = e.tensor_copy(
                            den[:, j * 4:(j + 1) * 4].unsqueeze(2),
                            pvb[:, pv_base(j):pv_base(j) + 132].rearrange(
                                "p (q c) -> p q c", c=33)[:, :, 32:33])
                    p.inc(ins, "V")
                den_V = p.cnt["V"]

                @block.scalar
                def _(e, den_V=den_V):
                    p.w(e, "V", den_V)
                    e.activation(den[:], den[:], AF.Ln)
                    e.drain()
                    ins = e.activation(den[:], den[:], AF.Exp, scale=-1.0)
                    e.drain()
                    p.inc(ins, "S")
                dinv_S = p.cnt["S"]

                @block.vector
                def _(e, s=s, qc=qc, dinv_S=dinv_S):
                    p.w(e, "S", dinv_S)
                    g0 = st.m.get("outc_tp_T")
                    if g0 is not None and qc == 0:
                        p.w(e, "T", g0)
                    for j in range(4):
                        h4 = 4 * qc + j
                        for qt in range(4):
                            ins = e.tensor_scalar(
                                outc_tm[:, qt * 512 + h4 * 32: qt * 512 + (h4 + 1) * 32],
                                pvb[:, pv_base(j) + qt * 33: pv_base(j) + qt * 33 + 32],
                                den[:, j * 4 + qt: j * 4 + qt + 1], None, ALU.mult)
                    p.inc(ins, "V")
            outc_V = p.cnt["V"]

        with (
            nc.psum_tensor(f"ptpo{s}", [128, 4096], BF16) as tpo,
            nc.psum_tensor(f"pwo0_{s}", [128, 512], F32) as wo0,
            nc.psum_tensor(f"pwo1_{s}", [128, 512], F32) as wo1,
        ):
            ps_wo = [wo0, wo1]
            for qt in range(4):
                tt = s * 4 + qt

                @block.tensor
                def _(e, qt=qt, outc_V=outc_V):
                    p.w(e, "V", outc_V)
                    g = st.m.get("ofm_evac_V")
                    if g is not None:
                        p.w(e, "V", g)
                    for c in range(4):
                        ins_last = e.transpose(
                            tpo[:, c * 1024:c * 1024 + 128],
                            outc_tm[:, qt * 512 + c * 128: qt * 512 + (c + 1) * 128],
                            sb["cb"][:, CB_ID:CB_ID + 128])
                    p.inc(ins_last, "T")
                tp_T = p.cnt["T"]

                @block.vector
                def _(e, tp_T=tp_T):
                    p.w(e, "T", tp_T)
                    g = st.m.get("ofm_used_T")
                    if g is not None:
                        p.w(e, "T", g)
                    ins = e.tensor_copy(
                        outc_fm[:].rearrange("p (b n) -> p b n", n=128),
                        tpo[:].rearrange("p (b n) -> p b n", n=1024)[:, :, 0:128])
                    p.inc(ins, "V")
                st.m["ofm_evac_V"] = p.cnt["V"]
                ofm_V = p.cnt["V"]

                for dc in range(2):
                    @block.tensor
                    def _(e, dc=dc, ofm_V=ofm_V):
                        p.w(e, "V", ofm_V)
                        g = st.m.get(f"woev_{dc}")
                        if g is not None:
                            p.w(e, "V", g)
                        for c in range(4):
                            ins_last = e.matmul(
                                ps_wo[dc][:],
                                outc_fm[:, c * 128:(c + 1) * 128],
                                b["wo"][:, c * 1024 + dc * 512:
                                        c * 1024 + (dc + 1) * 512],
                                start=(c == 0), stop=(c == 3))
                        p.inc(ins_last, "T")
                    st.m["ofm_used_T"] = p.cnt["T"]
                    wo_T = p.cnt["T"]

                    @block.vector
                    def _(e, tt=tt, dc=dc, wo_T=wo_T):
                        p.w(e, "T", wo_T)
                        xd = sb["x_tm"][:, tt * D + dc * 512: tt * D + (dc + 1) * 512]
                        ins = e.tensor_tensor(xd, xd, ps_wo[dc][:], ALU.add)
                        p.inc(ins, "V")
                    st.m[f"woev_{dc}"] = p.cnt["V"]
            st.m["strip_T_end"] = p.cnt["T"]
            st.m["outc_tp_T"] = p.cnt["T"]


def emit_out(st):
    p, block = st.p, st.block
    xr = st.m["x_ready_V"]

    @block.sync
    def _(e):
        p.w(e, "V", xr)
        for g in range(4):
            src = st.sb["x_tm"][:, 4 * g * D:(4 * g + 4) * D].rearrange(
                "p (n d) -> p n d", d=D)
            dst = st.io["out"].rearrange("(n p) d -> p n d", p=128)[:, 4 * g:4 * (g + 1), :]
            p.inc(e.dma_start(out=dst, in_=src), "O", 16)
        p.w(e, "O")


def _host_prep(inputs):
    x = np.ascontiguousarray(np.asarray(inputs["L0_attn_x"], dtype=np.float32))
    Wq = np.asarray(inputs["L0_attn_Wq"], dtype=np.float32)
    Wkv = np.ascontiguousarray(np.asarray(inputs["L0_attn_Wkv"], dtype=np.float32))
    Wk = np.ascontiguousarray(np.asarray(inputs["L0_attn_Wk"], dtype=np.float32))
    Wv = np.ascontiguousarray(np.asarray(inputs["L0_attn_Wv"], dtype=np.float32))
    Wo = np.asarray(inputs["L0_attn_Wo"], dtype=np.float32)
    q_g = np.asarray(inputs["L0_attn_q_g"], dtype=np.float32)
    k_g = np.asarray(inputs["L0_attn_k_g"], dtype=np.float32)
    gate_w = np.asarray(inputs["L0_ffn_gate_w"], dtype=np.float32)
    gate_b = np.asarray(inputs["L0_ffn_gate_bias"], dtype=np.float32)
    Wg = np.asarray(inputs["L0_ffn_Wg"], dtype=np.float32)
    Wu = np.asarray(inputs["L0_ffn_Wu"], dtype=np.float32)
    Wd = np.asarray(inputs["L0_ffn_Wd"], dtype=np.float32)
    Wsg = np.asarray(inputs["L0_ffn_Ws_g"], dtype=np.float32)
    Wsu = np.asarray(inputs["L0_ffn_Ws_u"], dtype=np.float32)
    Wsd = np.asarray(inputs["L0_ffn_Ws_d"], dtype=np.float32)

    slots = []
    for e in range(16):
        slots.append(np.concatenate([
            Wg[e].T.reshape(-1), Wu[e].T.reshape(-1), Wd[e].T.reshape(-1)]))
    for hf in range(2):
        sl = slice(hf * 512, (hf + 1) * 512)
        slots.append(np.concatenate([
            Wsg[sl, :].T.reshape(-1), Wsu[sl, :].T.reshape(-1),
            Wsd[:, sl].T.reshape(-1)]))
    wblob = np.ascontiguousarray(np.stack(slots))

    gwT = np.ascontiguousarray(gate_w.T)

    order = []
    for qc in range(4):
        for j in range(4):
            h = 4 * qc + j
            order += list(range(h * 64, h * 64 + 32))
    for qc in range(4):
        for j in range(4):
            h = 4 * qc + j
            order += list(range(h * 64 + 32, h * 64 + 64))
    wq = np.ascontiguousarray(Wq[:, order])
    wo_sub = np.ascontiguousarray(Wo.reshape(16, 64, D)[:, 0:32, :].reshape(512, D))

    cf = np.zeros((128, CF_COLS), dtype=np.float32)
    cf[:, CF_GB:CF_GB + 16] = gate_b[None, :]
    cf[:, CF_QG] = np.tile(q_g[:32], 4)
    cf[:, CF_KG] = np.tile(k_g, 4)
    pidx = np.arange(128)
    for c in range(8):
        head = 4 * (c % 4) + pidx // 32
        cf[pidx, CF_INDQ + c * 16 + head] = 1.0
    for c in range(4):
        head = 4 * c + pidx // 32
        cf[pidx, CF_INDK + c * 16 + head] = 1.0
    for qc in range(4):
        for h in range(4 * qc, 4 * qc + 4):
            r0 = 32 * (h % 4)
            cf[h, CF_SEL + qc * 128 + r0: CF_SEL + qc * 128 + r0 + 32] = 1.0
    cf[0:16, CF_I16:CF_I16 + 16] = np.eye(16, dtype=np.float32)

    cb = np.eye(128, dtype=np.float32)
    ki = np.arange(128)[:, None]
    qj = np.arange(128)[None, :]
    cbm = (qj >= ki).astype(np.float32)

    shared = dict(wblob=wblob, gwT=gwT, wq=wq, wkv=Wkv, wk=Wk, wv=Wv,
                  wo=wo_sub, cf=cf, cb=cb, cbm=cbm)
    return x, shared


def kernel(**inputs):
    x, shared = _host_prep(inputs)
    nc = build_program()
    in_maps = []
    for bb in range(8):
        m = dict(shared)
        m["x"] = np.ascontiguousarray(x[bb])
        in_maps.append(m)
    res = run_bass_kernel_spmd(nc, in_maps, core_ids=list(range(8)))
    out = np.stack([np.asarray(r["out"], dtype=np.float32) for r in res.results])
    return out


# version 14
# speedup vs baseline: 1.0587x; 1.0587x over previous
import math
import os as _os
import numpy as np
K_DBG = int(_os.environ.get("K_DBG", "0"))
from contextlib import ExitStack

import concourse.bass as bass
import concourse.mybir as mybir
from concourse.bass_utils import run_bass_kernel_spmd

F32 = mybir.dt.float32
BF16 = mybir.dt.bfloat16
AF = mybir.ActivationFunctionType
ALU = mybir.AluOpType
AX = mybir.AxisListType

T = 2048
D = 1024
NT = 16
NTS = 4
FC = 8
NE = 18
IC = 4
BIG = 1.0e30
LN_EPS = 1e-5
RMS_EPS = 1e-6

SCALAR_CONSTS = [LN_EPS, RMS_EPS, 1e-20, 0.5, -0.5, -1.0, math.log(1.25), 0.02]

SLOT_COLS = 4096 * 3
WBLOB_PER = D * 512 * 3

CF_GB = 0
CF_QG = 16
CF_KG = 17
CF_I16 = 18
CF_COLS = CF_I16 + 16

CB_ID = 0
CB_INDQ = 128
CB_INDK = CB_INDQ + 128
CB_SEL = CB_INDK + 64
CB_COLS = CB_SEL + 512

MI_QG8 = 0
MI_MQ = 1
MI_MK = 2
MI_MH = 3
MI_NEGM = 8
MI_NEGMC = 24

LN_SUMA, LN_SUMB, LN_SQA, LN_SQB, LN_INV, LN_B, LN_A2, LN_B2, LN_MU, LN_VAR, LN_T1, LN_T2 = (
    0, 16, 32, 48, 64, 80, 96, 112, 128, 144, 160, 176)


class Prog:
    def __init__(self):
        self.nc = bass.Bass("TRN2", target_bir_lowering=False)
        self.es = ExitStack()
        self.cnt = {}
        self.sems = {}
        self.mark = {}

    def sem(self, name):
        self.sems[name] = self.es.enter_context(self.nc.semaphore(name))
        self.cnt[name] = 0

    def inc(self, instr, name, amt=1):
        instr.then_inc(self.sems[name], amt)
        self.cnt[name] += amt
        return self.cnt[name]

    def w(self, eng, name, val=None):
        v = self.cnt[name] if val is None else val
        if v > 0:
            eng.wait_ge(self.sems[name], v)


def build_program():
    p = Prog()
    nc = p.nc
    es = p.es
    io = {}
    io["x"] = nc.dram_tensor("x", [T, D], F32, kind="ExternalInput").ap()
    io["wblob"] = nc.dram_tensor("wblob", [NE, WBLOB_PER], F32, kind="ExternalInput").ap()
    io["gwT"] = nc.dram_tensor("gwT", [D, 16], F32, kind="ExternalInput").ap()
    io["wq"] = nc.dram_tensor("wq", [D, D], F32, kind="ExternalInput").ap()
    io["wkv"] = nc.dram_tensor("wkv", [D, 128], F32, kind="ExternalInput").ap()
    io["wk"] = nc.dram_tensor("wk", [128, 512], F32, kind="ExternalInput").ap()
    io["wv"] = nc.dram_tensor("wv", [128, 512], F32, kind="ExternalInput").ap()
    io["wo"] = nc.dram_tensor("wo", [512, D], F32, kind="ExternalInput").ap()
    io["cf"] = nc.dram_tensor("cf", [128, CF_COLS], F32, kind="ExternalInput").ap()
    io["cb"] = nc.dram_tensor("cb", [128, CB_COLS], F32, kind="ExternalInput").ap()
    io["cbm"] = nc.dram_tensor("cbm", [128, 128], F32, kind="ExternalInput").ap()
    io["out"] = nc.dram_tensor("out", [T, D], F32, kind="ExternalOutput").ap()

    sb = {}
    sb["x_tm"] = es.enter_context(nc.sbuf_tensor("x_tm", [128, NT * D], F32))
    sb["ring"] = es.enter_context(nc.sbuf_tensor("ring", [128, 2 * SLOT_COLS], BF16))
    sb["gatew"] = es.enter_context(nc.sbuf_tensor("gatew", [128, 8 * 16], BF16))
    sb["cf"] = es.enter_context(nc.sbuf_tensor("cfsb", [128, CF_COLS], F32))
    sb["cb"] = es.enter_context(nc.sbuf_tensor("cbsb", [128, CB_COLS], BF16))
    sb["cpool"] = es.enter_context(nc.sbuf_tensor("cpool", [128, len(SCALAR_CONSTS)], F32))
    sb["misc"] = es.enter_context(nc.sbuf_tensor("miscsb", [128, 32], F32))
    sb["lnst"] = es.enter_context(nc.sbuf_tensor("lnst", [128, 192], F32))
    sb["lndump"] = es.enter_context(nc.sbuf_tensor("lndump", [128, 512], F32))

    for s in ("DX", "DW", "T", "V", "S", "WF", "O"):
        p.sem(s)

    st = Ctx(p, io, sb)
    st.block = es.enter_context(nc.Block())
    emit_init(st)
    emit_ffn(st, 0)
    emit_attention(st)
    emit_ffn(st, 1)
    emit_out(st)
    es.close()
    return nc


class Ctx:
    def __init__(self, p, io, sb):
        self.p = p
        self.nc = p.nc
        self.io = io
        self.sb = sb
        self.m = p.mark


def emit_init(st):
    p, block, io, sb = st.p, st.block, st.io, st.sb

    @block.sync
    def _(e):
        for g in range(4):
            src = io["x"].rearrange("(n p) d -> p n d", p=128)[:, 4 * g:4 * (g + 1), :]
            dst = sb["x_tm"][:, 4 * g * D:(4 * g + 4) * D].rearrange("p (n d) -> p n d", d=D)
            p.inc(e.dma_start(out=dst, in_=src), "DX", 16)
        p.inc(e.dma_start(out=sb["cf"][:], in_=io["cf"]), "DX", 16)
    st.m["xin_DX"] = p.cnt["DX"]
    st.m["xin_groups"] = [16 * (g + 1) for g in range(4)]

    @block.gpsimd
    def _(e):
        p.inc(e.dma_start(out=sb["cb"][:], in_=io["cb"]), "DW", 16)
        p.inc(e.dma_start(out=sb["gatew"][:].rearrange("p (c n) -> p c n", n=16),
                          in_=io["gwT"].rearrange("(c p) n -> p c n", p=128)), "DW", 16)
    st.m["init_DW"] = p.cnt["DW"]

    @block.vector
    def _(e):
        for i, v in enumerate(SCALAR_CONSTS):
            e.memset(sb["cpool"][:, i:i + 1], v)
        p.w(e, "DX")
        ins = e.tensor_scalar(sb["misc"][:, MI_QG8:MI_QG8 + 1],
                              sb["cf"][:, CF_QG:CF_QG + 1], 0.125, None, ALU.mult)
        e.drain()
        p.inc(ins, "V")
    st.m["consts_V"] = p.cnt["V"]
    for i, v in enumerate(SCALAR_CONSTS):
        st.nc.const_aps.register(F32, v, sb["cpool"][:, i:i + 1])


def emit_ln(st, kind, xn_fm, dump, xn_tmp, ps_tp):
    p, block, sb = st.p, st.block, st.sb
    lnst = sb["lnst"]
    x_tm = sb["x_tm"]

    @block.scalar
    def _(e):
        for tt in range(NT):
            mk = st.m.get(f"xfin_{tt}")
            if mk is not None:
                p.w(e, "V", mk)
            else:
                p.w(e, "DX", st.m["xin_groups"][tt // 4])
            xt = x_tm[:, tt * D:(tt + 1) * D]
            e.activation(dump[:], xt[:, 0:512], AF.Copy,
                         accum_out=lnst[:, LN_SUMA + tt:LN_SUMA + tt + 1])
            e.activation(dump[:], xt[:, 512:1024], AF.Copy,
                         accum_out=lnst[:, LN_SUMB + tt:LN_SUMB + tt + 1])
            e.activation(dump[:], xt[:, 0:512], AF.Square,
                         accum_out=lnst[:, LN_SQA + tt:LN_SQA + tt + 1])
            ins = e.activation(dump[:], xt[:, 512:1024], AF.Square,
                               accum_out=lnst[:, LN_SQB + tt:LN_SQB + tt + 1])
        p.inc(ins, "S")
    stats_S = p.cnt["S"]

    @block.vector
    def _(e):
        p.w(e, "S", stats_S)
        mu = lnst[:, LN_MU:LN_MU + 16]
        var = lnst[:, LN_VAR:LN_VAR + 16]
        t1 = lnst[:, LN_T1:LN_T1 + 16]
        e.tensor_tensor(mu, lnst[:, LN_SUMA:LN_SUMA + 16],
                        lnst[:, LN_SUMB:LN_SUMB + 16], ALU.add)
        e.drain()
        e.tensor_scalar(mu, mu, 1.0 / D, None, ALU.mult)
        e.drain()
        e.tensor_tensor(var, lnst[:, LN_SQA:LN_SQA + 16],
                        lnst[:, LN_SQB:LN_SQB + 16], ALU.add)
        e.drain()
        e.tensor_scalar(var, var, 1.0 / D, None, ALU.mult)
        e.drain()
        e.tensor_tensor(t1, mu, mu, ALU.mult)
        e.drain()
        ins = e.tensor_tensor(var, var, t1, ALU.subtract)
        e.drain()
        p.inc(ins, "V")
    var_V = p.cnt["V"]

    @block.scalar
    def _(e):
        p.w(e, "V", var_V)
        e.activation(lnst[:, LN_T2:LN_T2 + 16], lnst[:, LN_VAR:LN_VAR + 16],
                     AF.Ln, bias=LN_EPS)
        e.drain()
        ins = e.activation(lnst[:, LN_INV:LN_INV + 16], lnst[:, LN_T2:LN_T2 + 16],
                           AF.Exp, scale=-0.5)
        e.drain()
        p.inc(ins, "S")
    inv_S = p.cnt["S"]

    @block.vector
    def _(e):
        p.w(e, "S", inv_S)
        inv = lnst[:, LN_INV:LN_INV + 16]
        mu = lnst[:, LN_MU:LN_MU + 16]
        e.scalar_tensor_tensor(lnst[:, LN_B:LN_B + 16], mu, -1.0, inv, ALU.mult, ALU.mult)
        e.drain()
        if kind == "ffn":
            e.tensor_scalar(lnst[:, LN_A2:LN_A2 + 16], inv, 0.5, 1.0, ALU.mult, ALU.add)
            e.tensor_scalar(lnst[:, LN_B2:LN_B2 + 16], lnst[:, LN_B:LN_B + 16],
                            0.5, None, ALU.mult)
            e.drain()
        p.inc(e.nop(), "V")
    coef_V = p.cnt["V"]

    evac_marks = {}
    grp_tp_last = {}
    grp_S = {}
    step = 0
    for g in range(8):
        @block.scalar
        def _(e, g=g):
            p.w(e, "V", coef_V)
            if g > 0:
                p.w(e, "T", grp_tp_last[g - 1])
            for ti in range(2):
                tt = 2 * g + ti
                xt = x_tm[:, tt * D:(tt + 1) * D]
                ins_last = e.activation(xn_tmp[:, ti * D:(ti + 1) * D], xt, AF.Identity,
                                        scale=lnst[:, LN_INV + tt:LN_INV + tt + 1],
                                        bias=lnst[:, LN_B + tt:LN_B + tt + 1])
                if kind == "ffn":
                    ins_last = e.activation(xt, xt, AF.Identity,
                                            scale=lnst[:, LN_A2 + tt:LN_A2 + tt + 1],
                                            bias=lnst[:, LN_B2 + tt:LN_B2 + tt + 1])
            p.inc(ins_last, "S")
        grp_S[g] = p.cnt["S"]

        for fc in range(FC):
            pp = step % 2

            @block.tensor
            def _(e, g=g, fc=fc, pp=pp, step=step):
                if step == 0:
                    p.w(e, "DW", st.m["init_DW"])
                p.w(e, "S", grp_S[g])
                if step >= 2:
                    p.w(e, "V", evac_marks[step - 2])
                for ti in range(2):
                    ins_last = e.transpose(ps_tp[pp][:, ti * 1024:ti * 1024 + 128],
                                xn_tmp[:, ti * D + fc * 128: ti * D + (fc + 1) * 128],
                                sb["cb"][:, CB_ID:CB_ID + 128])
                p.inc(ins_last, "T")
            tp_T = p.cnt["T"]

            @block.vector
            def _(e, g=g, fc=fc, pp=pp, tp_T=tp_T):
                p.w(e, "T", tp_T)
                ins = e.tensor_copy(
                    xn_fm[:, fc * T + g * 256: fc * T + (g + 1) * 256]
                    .rearrange("p (b n) -> p b n", n=128),
                    ps_tp[pp][:].rearrange("p (b n) -> p b n", n=1024)[:, :, 0:128])
                p.inc(ins, "V")
            evac_marks[step] = p.cnt["V"]
            step += 1
        grp_tp_last[g] = p.cnt["T"]

    st.m["xn_T"] = p.cnt["T"]
    st.m["xn_V"] = p.cnt["V"]


def emit_ffn(st, call):
    p, nc = st.p, st.nc
    with (
        nc.sbuf_tensor(f"xnf{call}", [128, FC * T], BF16) as xn_fm,
        nc.sbuf_tensor(f"cwh{call}", [128, 256], F32) as cwh,
    ):
        with (
            nc.sbuf_tensor(f"xnt{call}", [128, 2 * D], BF16) as xn_tmp,
            nc.psum_tensor(f"ptpa{call}", [128, 2048], BF16) as tp_a,
            nc.psum_tensor(f"ptpb{call}", [128, 2048], BF16) as tp_b,
        ):
            emit_ln(st, "ffn", xn_fm, st.sb["lndump"], xn_tmp, [tp_a, tp_b])
        with (
            nc.psum_tensor(f"prt{call}", [128, 512], F32) as ps_rt,
            nc.sbuf_tensor(f"rts{call}", [128, 256], F32) as rt_s,
            nc.sbuf_tensor(f"rtf{call}", [128, 256], F32) as rt_sfc,
            nc.sbuf_tensor(f"rtt{call}", [128, 384], F32) as rt_t,
            nc.sbuf_tensor(f"rtg{call}", [128, 64], F32) as rt_g,
            nc.sbuf_tensor(f"rtm{call}", [128, 128], F32) as rt_m8,
        ):
            emit_routing(st, xn_fm, ps_rt, rt_s, rt_sfc, rt_t, rt_g, rt_m8, cwh)
        with (
            nc.sbuf_tensor(f"act{call}", [128, IC * T], BF16) as act_fm,
            nc.sbuf_tensor(f"sgt{call}", [128, 2 * 512], F32) as sg_tmp,
            nc.psum_tensor(f"pga{call}", [128, 512], F32) as g_a,
            nc.psum_tensor(f"pgb{call}", [128, 512], F32) as g_b,
            nc.psum_tensor(f"pua{call}", [128, 512], F32) as u_a,
            nc.psum_tensor(f"pub{call}", [128, 512], F32) as u_b,
            nc.psum_tensor(f"pda{call}", [128, 512], F32) as d_a,
            nc.psum_tensor(f"pdb{call}", [128, 512], F32) as d_b,
        ):
            bufs = dict(xn=xn_fm, act=act_fm, sg=sg_tmp, cwh=cwh,
                        ps_g=[g_a, g_b], ps_u=[u_a, u_b], ps_d=[d_a, d_b])
            for ei in range(NE):
                emit_expert(st, call, ei, bufs)
    st.m["x_ready_V"] = p.cnt["V"]


def emit_routing(st, xn_fm, ps_rt, rt_s, rt_sfc, rt_t, rt_g, rt_m8, cwh):
    p, block, sb = st.p, st.block, st.sb

    @block.tensor
    def _(e):
        p.w(e, "DW", st.m["init_DW"])
        for tt in range(NT):
            for fc in range(FC):
                ins_last = e.matmul(ps_rt[:, tt * 16:(tt + 1) * 16],
                         xn_fm[:, fc * T + tt * 128: fc * T + (tt + 1) * 128],
                         sb["gatew"][:, fc * 16:(fc + 1) * 16],
                         start=(fc == 0), stop=(fc == FC - 1))
        p.inc(ins_last, "T")
    rt_T = p.cnt["T"]

    @block.scalar
    def _(e):
        p.w(e, "T", rt_T)
        p.inc(e.activation(rt_s[:], ps_rt[:, 0:256], AF.Sigmoid), "S")
    sig_S = p.cnt["S"]
    st.m["rt_sig_S"] = sig_S

    @block.vector
    def _(e):
        p.w(e, "S", sig_S)
        e.drain()
        gb = sb["cf"][:, CF_GB:CF_GB + 16].unsqueeze(1).to_broadcast(
            [128, 16, 16])
        e.tensor_tensor(rt_sfc[:].rearrange("p (t e) -> p t e", e=16),
                        rt_s[:].rearrange("p (t e) -> p t e", e=16), gb, ALU.add)
        e.drain()
        v4 = rt_sfc[:].rearrange("p (q e) -> p q e", e=4)
        for i, (a, b) in enumerate([(0, 1), (0, 2), (0, 3), (1, 2), (1, 3), (2, 3)]):
            e.tensor_tensor(rt_t[:, i * 64:(i + 1) * 64], v4[:, :, a], v4[:, :, b], ALU.add)
        e.drain()
        e.tensor_tensor(rt_g[:], rt_t[:, 0:64], rt_t[:, 64:128], ALU.max)
        e.drain()
        for i in range(2, 6):
            e.tensor_tensor(rt_g[:], rt_g[:], rt_t[:, i * 64:(i + 1) * 64], ALU.max)
        e.drain()
        gv = rt_g[:].rearrange("p (t g) -> p t g", g=4)
        t0, t1, t2, t3 = (rt_t[:, i * 16:(i + 1) * 16] for i in range(4))
        thr = rt_t[:, 64:80]
        e.tensor_tensor(t0, gv[:, :, 0], gv[:, :, 1], ALU.max)
        e.drain()
        e.tensor_tensor(t1, gv[:, :, 2], gv[:, :, 3], ALU.max)
        e.drain()
        e.tensor_tensor(t2, gv[:, :, 0], gv[:, :, 1], ALU.min)
        e.drain()
        e.tensor_tensor(t3, gv[:, :, 2], gv[:, :, 3], ALU.min)
        e.drain()
        e.tensor_tensor(thr, t0, t1, ALU.min)
        e.drain()
        e.tensor_tensor(thr, thr, t2, ALU.max)
        e.drain()
        e.tensor_tensor(thr, thr, t3, ALU.max)
        e.drain()
        pen = rt_t[:, 80:144]
        thrb = thr.unsqueeze(2).to_broadcast([128, 16, 4])
        e.tensor_tensor(pen.rearrange("p (t g) -> p t g", g=4), gv, thrb, ALU.is_lt)
        e.drain()
        e.tensor_scalar(pen, pen, -BIG, None, ALU.mult)
        e.drain()
        penb = pen.rearrange("p (t g) -> p t g", g=4).unsqueeze(3).to_broadcast([128, 16, 4, 4])
        sfcv = rt_sfc[:].rearrange("p (t g e) -> p t g e", g=4, e=4)
        e.tensor_tensor(sfcv, sfcv, penb, ALU.add)
        e.drain()
        for tt in range(NT):
            e.max(out=rt_m8[:, tt * 8:(tt + 1) * 8], in_=rt_sfc[:, tt * 16:(tt + 1) * 16])
        e.drain()
        thr4b = rt_m8[:].rearrange("p (t e) -> p t e", e=8)[:, :, 3:4].to_broadcast(
            [128, 16, 16])
        sel = rt_t[:, 128:384]
        e.tensor_tensor(sel.rearrange("p (t e) -> p t e", e=16),
                        rt_sfc[:].rearrange("p (t e) -> p t e", e=16), thr4b, ALU.is_ge)
        e.drain()
        e.tensor_tensor(sel, sel, rt_s[:], ALU.mult)
        sv = sel.rearrange("p (t e) -> p t e", e=16)
        e.drain()
        e.tensor_tensor(rt_t[:, 0:128].rearrange("p (t e) -> p t e", e=8),
                        sv[:, :, 0:8], sv[:, :, 8:16], ALU.add)
        e.drain()
        u1 = rt_t[:, 0:128].rearrange("p (t e) -> p t e", e=8)
        e.tensor_tensor(rt_g[:, 0:64].rearrange("p (t e) -> p t e", e=4),
                        u1[:, :, 0:4], u1[:, :, 4:8], ALU.add)
        e.drain()
        u2 = rt_g[:, 0:64].rearrange("p (t e) -> p t e", e=4)
        e.tensor_tensor(rt_t[:, 0:32].rearrange("p (t e) -> p t e", e=2),
                        u2[:, :, 0:2], u2[:, :, 2:4], ALU.add)
        e.drain()
        u3 = rt_t[:, 0:32].rearrange("p (t e) -> p t e", e=2)
        ins = e.tensor_tensor(rt_g[:, 0:16].unsqueeze(2),
                              u3[:, :, 0:1], u3[:, :, 1:2], ALU.add)
        e.drain()
        p.inc(ins, "V")
    wsum_V = p.cnt["V"]

    @block.scalar
    def _(e):
        p.w(e, "V", wsum_V)
        e.activation(rt_g[:, 16:32], rt_g[:, 0:16], AF.Ln, bias=1e-20)
        e.drain()
        ins = e.activation(rt_g[:, 32:48], rt_g[:, 16:32], AF.Exp,
                           scale=-1.0, bias=math.log(1.25))
        e.drain()
        p.inc(ins, "S")
    rf_S = p.cnt["S"]

    @block.vector
    def _(e):
        p.w(e, "S", rf_S)
        rfb = rt_g[:, 32:48].unsqueeze(2).to_broadcast([128, 16, 16])
        sel = rt_t[:, 128:384]
        ins = e.tensor_tensor(cwh[:].rearrange("p (t e) -> p t e", e=16),
                              sel.rearrange("p (t e) -> p t e", e=16), rfb, ALU.mult)
        p.inc(ins, "V")
    st.m["cwh_V"] = p.cnt["V"]


def emit_expert(st, call, ei, bufs):
    p, block, sb = st.p, st.block, st.sb
    slot = ei % 2
    base = slot * SLOT_COLS
    ring = sb["ring"]
    wg = ring[:, base:base + 4096]
    wu = ring[:, base + 4096:base + 8192]
    wd = ring[:, base + 8192:base + 12288]
    xn, act, sg, cwh = bufs["xn"], bufs["act"], bufs["sg"], bufs["cwh"]
    ps_g, ps_u, ps_d = bufs["ps_g"], bufs["ps_u"], bufs["ps_d"]

    @block.gpsimd
    def _(e):
        pf = st.m.get(f"slotfree_{slot}")
        if pf is not None:
            p.w(e, "WF", pf)
        src = st.io["wblob"][ei]
        for half in range(2):
            sap = src[half * 262144:(half + 1) * 262144].rearrange(
                "(c p n) -> p c n", p=128, n=512)
            p.inc(e.dma_start(out=wg[:, half * 2048:(half + 1) * 2048]
                              .rearrange("p (c n) -> p c n", n=512), in_=sap), "DW", 16)
            sap = src[524288 + half * 262144: 524288 + (half + 1) * 262144].rearrange(
                "(c p n) -> p c n", p=128, n=512)
            p.inc(e.dma_start(out=wu[:, half * 2048:(half + 1) * 2048]
                              .rearrange("p (c n) -> p c n", n=512), in_=sap), "DW", 16)
            sap = src[1048576 + half * 262144: 1048576 + (half + 1) * 262144].rearrange(
                "(c p n) -> p c n", p=128, n=1024)
            p.inc(e.dma_start(out=wd[:, half * 2048:(half + 1) * 2048]
                              .rearrange("p (c n) -> p c n", n=1024), in_=sap), "DW", 16)
    w_DW = p.cnt["DW"]

    for n in range(NTS):
        for m in range(IC):
            pp = (n * IC + m) % 2

            @block.tensor
            def _(e, n=n, m=m, pp=pp):
                p.w(e, "DW", w_DW)
                if ei == 0 and n == 0 and m == 0:
                    p.w(e, "V", st.m["xn_V"])
                    p.w(e, "S", st.m["rt_sig_S"])
                g = st.m.get(f"up_evac_{pp}")
                if g is not None:
                    p.w(e, "V", g)
                for fc in range(FC):
                    e.matmul(ps_g[pp][:],
                             wg[:, fc * 512 + m * 128: fc * 512 + (m + 1) * 128],
                             xn[:, fc * T + n * 512: fc * T + (n + 1) * 512],
                             start=(fc == 0), stop=(fc == FC - 1))
                for fc in range(FC):
                    ins_last = e.matmul(ps_u[pp][:],
                             wu[:, fc * 512 + m * 128: fc * 512 + (m + 1) * 128],
                             xn[:, fc * T + n * 512: fc * T + (n + 1) * 512],
                             start=(fc == 0), stop=(fc == FC - 1))
                p.inc(ins_last, "T")
            up_T = p.cnt["T"]

            @block.scalar
            def _(e, pp=pp, up_T=up_T):
                p.w(e, "T", up_T)
                g = st.m.get(f"up_mul_{pp}")
                if g is not None:
                    p.w(e, "V", g)
                ins = e.activation(sg[:, pp * 512:(pp + 1) * 512], ps_g[pp][:], AF.Silu)
                p.inc(ins, "S")
            silu_S = p.cnt["S"]

            @block.vector
            def _(e, n=n, m=m, pp=pp, silu_S=silu_S):
                p.w(e, "S", silu_S)
                ins = e.tensor_tensor(act[:, m * T + n * 512: m * T + (n + 1) * 512],
                                      sg[:, pp * 512:(pp + 1) * 512], ps_u[pp][:],
                                      ALU.mult)
                p.inc(ins, "V")
            st.m[f"up_evac_{pp}"] = p.cnt["V"]
            st.m[f"up_mul_{pp}"] = p.cnt["V"]
    act_V = p.cnt["V"]

    for tt in range(NT):
        for dc in range(2):
            pp = (tt * 2 + dc) % 2

            @block.tensor
            def _(e, tt=tt, dc=dc, pp=pp):
                if tt == 0 and dc == 0:
                    p.w(e, "V", act_V)
                g = st.m.get(f"dn_evac_{pp}")
                if g is not None:
                    p.w(e, "V", g)
                for ic in range(IC):
                    ins_last = e.matmul(ps_d[pp][:],
                             act[:, ic * T + tt * 128: ic * T + (tt + 1) * 128],
                             wd[:, ic * 1024 + dc * 512: ic * 1024 + (dc + 1) * 512],
                             start=(ic == 0), stop=(ic == IC - 1))
                p.inc(ins_last, "T")
            dn_T = p.cnt["T"]

            @block.vector
            def _(e, tt=tt, dc=dc, pp=pp, dn_T=dn_T):
                p.w(e, "T", dn_T)
                if ei < 16:
                    if tt == 0 and dc == 0:
                        p.w(e, "V", st.m["cwh_V"])
                    sc = cwh[:, tt * 16 + ei: tt * 16 + ei + 1]
                else:
                    sc = 0.5
                xd = sb["x_tm"][:, tt * D + dc * 512: tt * D + (dc + 1) * 512]
                ins = e.scalar_tensor_tensor(xd, ps_d[pp][:], sc, xd, ALU.mult, ALU.add)
                p.inc(ins, "V")
            st.m[f"dn_evac_{pp}"] = p.cnt["V"]
            if ei == NE - 1 and dc == 1:
                st.m[f"xfin_{tt}"] = p.cnt["V"]

    @block.tensor
    def _(e):
        e.drain()
        p.inc(e.nop(), "WF")
    st.m[f"slotfree_{slot}"] = p.cnt["WF"]


def emit_attention(st):
    p, nc, sb = st.p, st.nc, st.sb
    ring = sb["ring"]
    wo = ring[:, 0:4096]
    wkv = ring[:, 4096:5120]
    wk = ring[:, 5120:5632]
    wv = ring[:, 5632:6144]
    wq = ring[:, 6144:14336]

    @st.block.gpsimd
    def _(e):
        p.w(e, "WF")
        p.inc(e.dma_start(out=wo.rearrange("p (c n) -> p c n", n=1024),
                          in_=st.io["wo"].rearrange("(c p) n -> p c n", p=128)), "DW", 16)
        p.inc(e.dma_start(out=wkv.rearrange("p (c n) -> p c n", n=128),
                          in_=st.io["wkv"].rearrange("(c p) n -> p c n", p=128)), "DW", 16)
        p.inc(e.dma_start(out=wk, in_=st.io["wk"]), "DW", 16)
        p.inc(e.dma_start(out=wv, in_=st.io["wv"]), "DW", 16)
        for half in range(2):
            p.inc(e.dma_start(
                out=wq[:, half * 4096:(half + 1) * 4096].rearrange(
                    "p (c n) -> p c n", n=1024),
                in_=st.io["wq"].rearrange("(c p) n -> p c n", p=128)
                [:, 4 * half:4 * (half + 1), :]), "DW", 16)
    st.m["aw_DW"] = p.cnt["DW"]

    with (
        nc.sbuf_tensor("qn", [128, 4 * T], BF16) as qn,
        nc.sbuf_tensor("kn", [128, 4 * T], BF16) as kn,
        nc.sbuf_tensor("vaug", [128, NT * 528], BF16) as vaug,
        nc.sbuf_tensor("sqt", [128, 512], BF16) as sq_tmp,
        nc.sbuf_tensor("irms", [128, 512], F32) as irms,
        nc.sbuf_tensor("irmsb", [128, 512], BF16) as irmsb,
    ):
        with (
            nc.sbuf_tensor("xnfa", [128, FC * T], BF16) as xn_fm,
        ):
            with (
                nc.sbuf_tensor("xnta", [128, 2 * D], BF16) as xn_tmp,
                nc.psum_tensor("ptpaa", [128, 2048], BF16) as tp_a,
                nc.psum_tensor("ptpab", [128, 2048], BF16) as tp_b,
            ):
                emit_ln(st, "attn", xn_fm, st.sb["lndump"], xn_tmp, [tp_a, tp_b])
            with (
                nc.sbuf_tensor("kvsb", [128, T], BF16) as kv_sb,
                nc.psum_tensor("pqa", [128, 512], F32) as ps_qa,
                nc.psum_tensor("pqb", [128, 512], F32) as ps_qb,
                nc.psum_tensor("pss", [128, 512], F32) as ps_ssq,
                nc.psum_tensor("prb", [128, 512], F32) as ps_rb,
            ):
                emit_attn_prep(st, dict(qn=qn, kn=kn, vaug=vaug, sq=sq_tmp, irms=irms,
                                        irmsb=irmsb, xn=xn_fm, kv=kv_sb,
                                        ps_q=[ps_qa, ps_qb], ps_ssq=ps_ssq, ps_rb=ps_rb,
                                        wq=wq, wkv=wkv, wk=wk, wv=wv))
        with (
            nc.sbuf_tensor("probs", [128, 8 * 512], BF16) as probs,
            nc.sbuf_tensor("outctm", [128, 4 * 512], BF16) as outc_tm,
            nc.sbuf_tensor("outcfm", [128, 512], BF16) as outc_fm,
            nc.sbuf_tensor("densb", [128, 16], F32) as den_sb,
            nc.sbuf_tensor("masks", [128, 128], BF16) as masks,
        ):
            @st.block.gpsimd
            def _(e):
                p.w(e, "T")
                p.inc(e.dma_start(out=masks[:], in_=st.io["cbm"]), "DW", 16)
            st.m["mask_DW"] = p.cnt["DW"]
            emit_attn_scores(st, dict(qn=qn, kn=kn, vaug=vaug, probs=probs,
                                      outc_tm=outc_tm, outc_fm=outc_fm,
                                      den=den_sb, masks=masks, wo=wo))

    @st.block.tensor
    def _(e):
        e.drain()
        p.inc(e.nop(), "WF")
    st.m["slotfree_0"] = p.cnt["WF"]
    st.m["slotfree_1"] = p.cnt["WF"]
    st.m["x_ready_V"] = p.cnt["V"]


def emit_attn_prep(st, b):
    p, block, sb = st.p, st.block, st.sb
    ps_q, ps_ssq, ps_rb = b["ps_q"], b["ps_ssq"], b["ps_rb"]
    aw = st.m["aw_DW"]

    for ts in range(NTS):
        t0 = p.cnt["T"]
        s0 = p.cnt["S"]
        order = []
        for fco in range(FC):
            order.append(("q", fco))
            if fco >= 1:
                order.append(("ssq", fco - 1))
        order.append(("ssq", FC - 1))
        t_of = {ent: t0 + i + 1 for i, ent in enumerate(order)}
        s_of_sq = {}
        s_after = {}
        sc = s0
        for fco in range(FC):
            sc += 1
            s_of_sq[fco] = sc
            if fco < 4:
                sc += 1
            s_after[fco] = sc
        rms_prev_V = st.m.get("rms_t0_V")

        @block.tensor
        def _(e, ts=ts, order=order, t_of=t_of, s_of_sq=s_of_sq, s_after=s_after,
              rms_prev_V=rms_prev_V, s0=s0):
            for ent in order:
                kind, fco = ent
                if kind == "q":
                    pp = fco % 2
                    if ts == 0 and fco == 0:
                        p.w(e, "DW", aw)
                        p.w(e, "V", st.m["xn_V"])
                    if fco >= 2:
                        p.w(e, "S", s_after[fco - 2])
                    elif ts > 0 and fco == 0:
                        p.w(e, "S", s0)
                    for fc in range(FC):
                        ins_last = e.matmul(
                            ps_q[pp][:],
                            b["wq"][:, fc * 1024 + fco * 128: fc * 1024 + (fco + 1) * 128],
                            b["xn"][:, fc * T + ts * 512: fc * T + (ts + 1) * 512],
                            start=(fc == 0), stop=(fc == FC - 1))
                    p.inc(ins_last, "T")
                else:
                    if fco == 0 and rms_prev_V is not None:
                        p.w(e, "V", rms_prev_V)
                    p.w(e, "S", s_of_sq[fco])
                    ins_last = e.matmul(
                        ps_ssq[0:16, 0:512],
                        sb["cb"][:, CB_INDQ + fco * 16: CB_INDQ + (fco + 1) * 16],
                        b["sq"][:], start=(fco == 0), stop=(fco == FC - 1))
                    p.inc(ins_last, "T")

        @block.scalar
        def _(e, ts=ts, t_of=t_of):
            for fco in range(FC):
                pp = fco % 2
                p.w(e, "T", t_of[("q", fco)])
                ins = e.activation(b["sq"][:], ps_q[pp][:], AF.Square)
                p.inc(ins, "S")
                if fco < 4:
                    ins = e.activation(
                        b["qn"][:, fco * T + ts * 512: fco * T + (ts + 1) * 512],
                        ps_q[pp][:], AF.Copy)
                    p.inc(ins, "S")
        assert p.cnt["T"] == t0 + len(order)
        assert p.cnt["S"] == s_after[FC - 1]
        ssq_T = p.cnt["T"]
        emit_rms_apply(st, b, ts, ssq_T, 64, b["qn"], b["qn"],
                       sb["misc"][:, MI_QG8:MI_QG8 + 1])
    st.m["qn_V"] = p.cnt["V"]
    qph_S = p.cnt["S"]

    t0k = p.cnt["T"]
    s0k = p.cnt["S"]

    @block.tensor
    def _(e, t0k=t0k, s0k=s0k, qph_S=qph_S):
        for ts in range(NTS):
            pp = ts % 2
            if ts == 0:
                p.w(e, "S", qph_S)
            if ts >= 2:
                p.w(e, "S", s0k + ts - 1)
            for fc in range(FC):
                ins_last = e.matmul(
                    ps_q[pp][:], b["wkv"][:, fc * 128:(fc + 1) * 128],
                    b["xn"][:, fc * T + ts * 512: fc * T + (ts + 1) * 512],
                    start=(fc == 0), stop=(fc == FC - 1))
            p.inc(ins_last, "T")

    @block.scalar
    def _(e, t0k=t0k):
        for ts in range(NTS):
            pp = ts % 2
            p.w(e, "T", t0k + ts + 1)
            ins = e.activation(b["kv"][:, ts * 512:(ts + 1) * 512], ps_q[pp][:], AF.Copy)
            p.inc(ins, "S")
    kv_S = p.cnt["S"]
    assert kv_S == s0k + 4

    for ts in range(NTS):
        t0 = p.cnt["T"]
        s0 = p.cnt["S"]
        order = [("k", 0), ("k", 1), ("ssq", 0), ("k", 2), ("ssq", 1), ("k", 3),
                 ("ssq", 2), ("ssq", 3)]
        t_of = {ent: t0 + i + 1 for i, ent in enumerate(order)}
        s_of_sq = {kc: s0 + 2 * kc + 1 for kc in range(4)}
        s_after = {kc: s0 + 2 * kc + 2 for kc in range(4)}
        rms_prev_V = st.m.get("rms_t0_V")

        @block.tensor
        def _(e, ts=ts, order=order, t_of=t_of, s_of_sq=s_of_sq, s_after=s_after,
              rms_prev_V=rms_prev_V, s0=s0, kv_S=kv_S):
            for ent in order:
                kind, kc = ent
                if kind == "k":
                    pp = kc % 2
                    if ts == 0 and kc == 0:
                        p.w(e, "S", kv_S)
                    if kc >= 2:
                        p.w(e, "S", s_after[kc - 2])
                    elif ts > 0 and kc == 0:
                        p.w(e, "S", s0)
                    ins_last = e.matmul(
                        ps_q[pp][:], b["wk"][:, kc * 128:(kc + 1) * 128],
                        b["kv"][:, ts * 512:(ts + 1) * 512], start=True, stop=True)
                    p.inc(ins_last, "T")
                else:
                    if kc == 0 and rms_prev_V is not None:
                        p.w(e, "V", rms_prev_V)
                    p.w(e, "S", s_of_sq[kc])
                    ins_last = e.matmul(
                        ps_ssq[0:16, 0:512],
                        sb["cb"][:, CB_INDK + kc * 16: CB_INDK + (kc + 1) * 16],
                        b["sq"][:], start=(kc == 0), stop=(kc == 3))
                    p.inc(ins_last, "T")

        @block.scalar
        def _(e, ts=ts, t_of=t_of):
            for kc in range(4):
                pp = kc % 2
                p.w(e, "T", t_of[("k", kc)])
                ins = e.activation(b["sq"][:], ps_q[pp][:], AF.Square)
                p.inc(ins, "S")
                ins = e.activation(
                    b["kn"][:, kc * T + ts * 512: kc * T + (ts + 1) * 512],
                    ps_q[pp][:], AF.Copy)
                p.inc(ins, "S")
        ssq_T = p.cnt["T"]
        emit_rms_apply(st, b, ts, ssq_T, 32, b["kn"], b["kn"],
                       sb["cf"][:, CF_KG:CF_KG + 1])
    st.m["kn_V"] = p.cnt["V"]
    kph_S = p.cnt["S"]

    v0v = p.cnt["V"]
    t0v = p.cnt["T"]

    @block.tensor
    def _(e, v0v=v0v, kph_S=kph_S):
        for tt in range(NT):
            pp = tt % 2
            if tt == 0:
                p.w(e, "S", kph_S)
                p.w(e, "V", st.m["kn_V"])
            if tt >= 2:
                p.w(e, "V", v0v + tt)
            ins_last = e.matmul(ps_q[pp][:], b["kv"][:, tt * 128:(tt + 1) * 128],
                                b["wv"], start=True, stop=True)
            p.inc(ins_last, "T")

    @block.vector
    def _(e, t0v=t0v):
        for tt in range(NT):
            ones = b["vaug"][:, tt * 528:(tt + 1) * 528].rearrange(
                "p (h c) -> p h c", c=33)[:, :, 32:33]
            ins = e.memset(ones, 1.0)
            if tt == 0:
                p.inc(ins, "V")
        for tt in range(NT):
            pp = tt % 2
            p.w(e, "T", t0v + tt + 1)
            dst = b["vaug"][:, tt * 528:(tt + 1) * 528].rearrange(
                "p (h c) -> p h c", c=33)[:, :, 0:32]
            src = ps_q[pp][:].rearrange("p (h c) -> p h c", c=32)
            ins = e.tensor_copy(dst, src)
            p.inc(ins, "V")
    assert p.cnt["V"] == v0v + 17

    emit_M(st, b)


def emit_rms_apply(st, b, ts, ssq_T, nd, raw, out, gcol):
    p, block, sb = st.p, st.block, st.sb
    ps_ssq, ps_rb = b["ps_ssq"], b["ps_rb"]

    @block.vector
    def _(e):
        p.w(e, "T", ssq_T)
        ins = e.tensor_scalar(b["irms"][0:16, 0:512], ps_ssq[0:16, 0:512],
                              1.0 / nd, RMS_EPS, ALU.mult, ALU.add)
        p.inc(ins, "V")
    t0_V = p.cnt["V"]
    st.m["rms_t0_V"] = t0_V

    @block.scalar
    def _(e):
        p.w(e, "V", t0_V)
        e.activation(b["irms"][0:16, 0:512], b["irms"][0:16, 0:512], AF.Ln)
        e.drain()
        ins = e.activation(b["irmsb"][0:16, 0:512], b["irms"][0:16, 0:512],
                           AF.Exp, scale=-0.5)
        p.inc(ins, "S")
    inv_S = p.cnt["S"]

    for qc in range(4):
        @block.tensor
        def _(e, qc=qc):
            p.w(e, "S", inv_S)
            g = st.m.get("rb_used_V")
            if g is not None:
                p.w(e, "V", g)
            ins_last = e.matmul(
                ps_rb[:], sb["cb"][0:16, CB_SEL + qc * 128: CB_SEL + (qc + 1) * 128],
                b["irmsb"][0:16, 0:512], start=True, stop=True)
            p.inc(ins_last, "T")
        rb_T = p.cnt["T"]

        @block.vector
        def _(e, qc=qc, rb_T=rb_T):
            p.w(e, "T", rb_T)
            ins = e.scalar_tensor_tensor(
                out[:, qc * T + ts * 512: qc * T + (ts + 1) * 512],
                ps_rb[:], gcol, raw[:, qc * T + ts * 512: qc * T + (ts + 1) * 512],
                ALU.mult, ALU.mult)
            p.inc(ins, "V")
        st.m["rb_used_V"] = p.cnt["V"]


def emit_M(st, b):
    p, block, sb = st.p, st.block, st.sb
    ps_ssq, ps_rb = b["ps_ssq"], b["ps_rb"]
    for which, buf, mcol in (("q", b["qn"], MI_MQ), ("k", b["kn"], MI_MK)):
        for ts in range(NTS):
            for qc in range(4):
                @block.vector
                def _(e, ts=ts, qc=qc, buf=buf, which=which):
                    g = st.m.get("msq_T")
                    if g is not None:
                        p.w(e, "T", g)
                    if which == "q" and ts == 0 and qc == 0:
                        p.w(e, "T")
                        p.w(e, "V", st.m["qn_V"])
                    if which == "k" and ts == 0 and qc == 0:
                        p.w(e, "V", st.m["kn_V"])
                    src = buf[:, qc * T + ts * 512: qc * T + (ts + 1) * 512]
                    ins = e.tensor_tensor(b["sq"][:], src, src, ALU.mult)
                    p.inc(ins, "V")
                msq_V = p.cnt["V"]

                @block.tensor
                def _(e, qc=qc, msq_V=msq_V):
                    p.w(e, "V", msq_V)
                    g = st.m.get("mred_V")
                    if g is not None and qc == 0:
                        p.w(e, "V", g)
                    ins_last = e.matmul(
                        ps_ssq[0:16, 0:512],
                        sb["cb"][:, CB_INDK + qc * 16: CB_INDK + (qc + 1) * 16],
                        b["sq"][:], start=(qc == 0), stop=(qc == 3))
                    p.inc(ins_last, "T")
                st.m["msq_T"] = p.cnt["T"]
            ssq_T = p.cnt["T"]

            @block.vector
            def _(e, ts=ts, ssq_T=ssq_T, mcol=mcol):
                p.w(e, "T", ssq_T)
                e.reduce_max(b["irms"][0:16, 0:1], ps_ssq[0:16, 0:512], axis=AX.X)
                e.drain()
                if ts == 0:
                    ins = e.tensor_copy(sb["misc"][0:16, mcol:mcol + 1],
                                        b["irms"][0:16, 0:1])
                else:
                    ins = e.tensor_tensor(sb["misc"][0:16, mcol:mcol + 1],
                                          sb["misc"][0:16, mcol:mcol + 1],
                                          b["irms"][0:16, 0:1], ALU.max)
                e.drain()
                p.inc(ins, "V")
            st.m["mred_V"] = p.cnt["V"]

    @block.vector
    def _(e):
        ins = e.tensor_tensor(sb["misc"][0:16, MI_MH:MI_MH + 1],
                              sb["misc"][0:16, MI_MQ:MI_MQ + 1],
                              sb["misc"][0:16, MI_MK:MI_MK + 1], ALU.mult)
        e.drain()
        p.inc(ins, "V")
    mh_V = p.cnt["V"]

    @block.scalar
    def _(e):
        p.w(e, "V", mh_V)
        e.activation(sb["misc"][0:16, MI_MH:MI_MH + 1],
                     sb["misc"][0:16, MI_MH:MI_MH + 1], AF.Ln)
        e.drain()
        ins = e.activation(sb["misc"][0:16, MI_MH:MI_MH + 1],
                           sb["misc"][0:16, MI_MH:MI_MH + 1], AF.Exp,
                           scale=0.5, bias=0.02)
        e.drain()
        p.inc(ins, "S")
    mh_S = p.cnt["S"]

    @block.vector
    def _(e):
        p.w(e, "S", mh_S)
        ins = e.tensor_copy(b["irms"][0:16, 0:128],
                            sb["misc"][0:16, MI_MH:MI_MH + 1].to_broadcast([16, 128]))
        e.drain()
        p.inc(ins, "V")
    mb_V = p.cnt["V"]

    @block.tensor
    def _(e):
        p.w(e, "V", mb_V)
        g = st.m.get("rb_used_V")
        if g is not None:
            p.w(e, "V", g)
        ins_last = e.matmul(ps_rb[:, 0:16], b["irms"][0:16, 0:128],
                 sb["cf"][0:16, CF_I16:CF_I16 + 16], start=True, stop=True)
        p.inc(ins_last, "T")
    nm_T = p.cnt["T"]

    @block.vector
    def _(e):
        p.w(e, "T", nm_T)
        e.tensor_scalar(sb["misc"][:, MI_NEGM:MI_NEGM + 16], ps_rb[:, 0:16],
                        -1.0, None, ALU.mult)
        e.drain()
        ng = sb["misc"][:].rearrange("p c -> p c 1") if False else None
        for qc in range(4):
            c = MI_NEGM + 4 * qc
            e.tensor_tensor(sb["misc"][:, MI_NEGMC + qc:MI_NEGMC + qc + 1],
                            sb["misc"][:, c:c + 1], sb["misc"][:, c + 1:c + 2], ALU.min)
            e.drain()
            e.tensor_tensor(sb["misc"][:, MI_NEGMC + qc:MI_NEGMC + qc + 1],
                            sb["misc"][:, MI_NEGMC + qc:MI_NEGMC + qc + 1],
                            sb["misc"][:, c + 2:c + 3], ALU.min)
            e.drain()
            ins = e.tensor_tensor(sb["misc"][:, MI_NEGMC + qc:MI_NEGMC + qc + 1],
                                  sb["misc"][:, MI_NEGMC + qc:MI_NEGMC + qc + 1],
                                  sb["misc"][:, c + 3:c + 4], ALU.min)
            e.drain()
        p.inc(ins, "V")
    st.m["negM_V"] = p.cnt["V"]


def emit_attn_scores(st, b):
    p, block, nc, sb = st.p, st.block, st.nc, st.sb
    probs, outc_tm, outc_fm, den = b["probs"], b["outc_tm"], b["outc_fm"], b["den"]

    def pv_base(j):
        return (j // 2) * 512 + (j % 2) * 256

    for s in range(4):
        n_kc = 4 * (s + 1)
        n_g = 2 * n_kc
        with (
            nc.psum_tensor(f"psca_{s}", [128, 1024], F32) as scbA,
            nc.psum_tensor(f"pscb_{s}", [128, 1024], F32) as scbB,
            nc.psum_tensor(f"ppv_{s}", [128, 1024], F32) as pvb,
        ):
            scb = [scbA, scbB]
            for qc in range(4):
                t0 = p.cnt["T"]
                s0 = p.cnt["S"]
                v0 = p.cnt["V"]

                order = [("sc", 0)]
                for g in range(1, n_g):
                    order.append(("sc", g))
                    order.append(("pv", g - 1))
                order.append(("pv", n_g - 1))
                t_map = {ent: t0 + i + 1 for i, ent in enumerate(order)}

                def t_sc(g):
                    return t_map[("sc", g)]

                def t_pv(g):
                    return t_map[("pv", g)]

                def s_exp(g):
                    return s0 + g + 1

                tri_v = {}
                vc = v0 + 1
                for kc in range(4 * s, n_kc):
                    for h in range(2):
                        vc += 1
                        tri_v[kc * 2 + h] = vc

                @block.vector
                def _(e, s=s, qc=qc, n_kc=n_kc, t0=t0, v0=v0, t_map=t_map,
                      s0=s0):
                    g0 = st.m.get("strip_T_end")
                    if g0 is not None and qc == 0:
                        p.w(e, "T", g0)
                    ins = e.memset(pvb[:], 0.0)
                    p.inc(ins, "V")
                    for kc in range(4 * s, n_kc):
                        d = kc - 4 * s
                        for h in range(2):
                            g = kc * 2 + h
                            p.w(e, "S", s0 + g + 1)
                            if s == 0 and qc == 0 and kc == 4 * s and h == 0:
                                p.w(e, "DW", st.m["mask_DW"])
                            pslot = kc % 2
                            dst = probs[:, (4 * h + pslot) * 512 + d * 128:]
                            dst = bass.AP(dst.tensor, dst.offset,
                                          [[4096, 128], [1024, 2], [1, 128]]) \
                                if False else probs[:].rearrange(
                                    "p (sl c) -> p sl c", c=512)[
                                    :, 4 * h + pslot:4 * h + pslot + 3:2,
                                    d * 128:(d + 1) * 128]
                            tri = b["masks"][:, 0:128].unsqueeze(1).to_broadcast(
                                [128, 2, 128])
                            if K_DBG & 1:
                                ins = e.nop()
                            else:
                                ins = e.tensor_tensor(dst, dst, tri, ALU.mult)
                            if K_DBG & 4 and s == 0 and qc == 0 and g == 0:
                                e.drain()
                                e.tensor_copy(sb["x_tm"][:, 0:512],
                                              probs[:, 0:512])
                                e.tensor_copy(sb["x_tm"][:, 512:640],
                                              b["masks"][:, 0:128])
                            p.inc(ins, "V")

                @block.tensor
                def _(e, s=s, qc=qc, n_kc=n_kc, n_g=n_g, t0=t0, s0=s0, v0=v0,
                      order=order):
                    for ent in order:
                        kind, g = ent
                        kc, h = g // 2, g % 2
                        pslot = kc % 2
                        if kind == "sc":
                            if g == 0:
                                if s == 0 and qc == 0:
                                    p.w(e, "V", st.m["negM_V"])
                                if qc == 0:
                                    for dcx in range(2):
                                        gm = st.m.get(f"woev_{dcx}")
                                        if gm is not None:
                                            p.w(e, "V", gm)
                                p.w(e, "S", s0)
                            if g >= 2:
                                p.w(e, "S", s_exp(g - 2))
                            for jj in range(2):
                                j = 2 * h + jj
                                ins_last = e.matmul(
                                    scb[g % 2][:, jj * 512:(jj + 1) * 512],
                                    b["kn"][32 * j:32 * (j + 1),
                                            qc * T + kc * 128: qc * T + (kc + 1) * 128],
                                    b["qn"][32 * j:32 * (j + 1),
                                            qc * T + s * 512: qc * T + (s + 1) * 512],
                                    start=True, stop=True,
                                    tile_position=(32 * j, 0))
                            p.inc(ins_last, "T")
                        else:
                            d = kc - 4 * s
                            p.w(e, "S", s_exp(g))
                            if d >= 0:
                                p.w(e, "V", tri_v[g])
                            if g == 0:
                                p.w(e, "V", v0 + 1)
                            ins_last = None
                            for jj in range(2):
                                j = 2 * h + jj
                                for qt in range(4):
                                    if (not (K_DBG & 2)) and d >= 0 and d > qt:
                                        continue
                                    ins_last = e.matmul(
                                        pvb[:, pv_base(j) + qt * 33:
                                            pv_base(j) + qt * 33 + 33],
                                        probs[:, (j * 2 + pslot) * 512 + qt * 128:
                                              (j * 2 + pslot) * 512 + (qt + 1) * 128],
                                        b["vaug"][:, kc * 528 + (4 * qc + j) * 33:
                                                  kc * 528 + (4 * qc + j) * 33 + 33],
                                        start=False,
                                        stop=(kc == ((n_kc - 1) if (K_DBG & 2)
                                              else min(n_kc - 1, 4 * s + qt))))
                            p.inc(ins_last, "T")

                @block.scalar
                def _(e, s=s, qc=qc, n_g=n_g, t0=t0, v0=v0):
                    for g in range(n_g):
                        kc, h = g // 2, g % 2
                        pslot = kc % 2
                        if g < 4:
                            p.w(e, "T", t0)
                        else:
                            p.w(e, "T", t_pv(g - 4))
                        p.w(e, "T", t_sc(g))
                        dst = probs[:].rearrange("p (sl c) -> p sl c", c=512)[
                            :, 4 * h + pslot:4 * h + pslot + 3:2, :]
                        ins = e.activation(
                            dst, scb[g % 2][:], AF.Exp,
                            bias=sb["misc"][:, MI_NEGMC + qc:MI_NEGMC + qc + 1])
                        p.inc(ins, "S")

                assert p.cnt["T"] == t0 + 2 * n_g
                assert p.cnt["S"] == s0 + n_g
                assert p.cnt["V"] == v0 + 9
                pv_T = p.cnt["T"]

                @block.vector
                def _(e, qc=qc, pv_T=pv_T):
                    p.w(e, "T", pv_T)
                    for j in range(4):
                        ins = e.tensor_copy(
                            den[:, j * 4:(j + 1) * 4].unsqueeze(2),
                            pvb[:, pv_base(j):pv_base(j) + 132].rearrange(
                                "p (q c) -> p q c", c=33)[:, :, 32:33])
                    p.inc(ins, "V")
                @block.vector
                def _(e, s=s, qc=qc):
                    e.drain()
                    e.reciprocal(den[:], den[:])
                    e.drain()
                    g0 = st.m.get("outc_tp_T")
                    if g0 is not None and qc == 0:
                        p.w(e, "T", g0)
                    for j in range(4):
                        h4 = 4 * qc + j
                        for qt in range(4):
                            ins = e.tensor_scalar(
                                outc_tm[:, qt * 512 + h4 * 32: qt * 512 + (h4 + 1) * 32],
                                pvb[:, pv_base(j) + qt * 33: pv_base(j) + qt * 33 + 32],
                                den[:, j * 4 + qt: j * 4 + qt + 1], None, ALU.mult)
                    p.inc(ins, "V")
            outc_V = p.cnt["V"]

        with (
            nc.psum_tensor(f"ptpo{s}", [128, 4096], BF16) as tpo,
            nc.psum_tensor(f"pwo0_{s}", [128, 512], F32) as wo0,
            nc.psum_tensor(f"pwo1_{s}", [128, 512], F32) as wo1,
        ):
            ps_wo = [wo0, wo1]
            for qt in range(4):
                tt = s * 4 + qt

                @block.tensor
                def _(e, qt=qt, outc_V=outc_V):
                    p.w(e, "V", outc_V)
                    g = st.m.get("ofm_evac_V")
                    if g is not None:
                        p.w(e, "V", g)
                    for c in range(4):
                        ins_last = e.transpose(
                            tpo[:, c * 1024:c * 1024 + 128],
                            outc_tm[:, qt * 512 + c * 128: qt * 512 + (c + 1) * 128],
                            sb["cb"][:, CB_ID:CB_ID + 128])
                    p.inc(ins_last, "T")
                tp_T = p.cnt["T"]

                @block.vector
                def _(e, tp_T=tp_T):
                    p.w(e, "T", tp_T)
                    g = st.m.get("ofm_used_T")
                    if g is not None:
                        p.w(e, "T", g)
                    ins = e.tensor_copy(
                        outc_fm[:].rearrange("p (b n) -> p b n", n=128),
                        tpo[:].rearrange("p (b n) -> p b n", n=1024)[:, :, 0:128])
                    p.inc(ins, "V")
                st.m["ofm_evac_V"] = p.cnt["V"]
                ofm_V = p.cnt["V"]

                for dc in range(2):
                    @block.tensor
                    def _(e, dc=dc, ofm_V=ofm_V):
                        p.w(e, "V", ofm_V)
                        g = st.m.get(f"woev_{dc}")
                        if g is not None:
                            p.w(e, "V", g)
                        for c in range(4):
                            ins_last = e.matmul(
                                ps_wo[dc][:],
                                outc_fm[:, c * 128:(c + 1) * 128],
                                b["wo"][:, c * 1024 + dc * 512:
                                        c * 1024 + (dc + 1) * 512],
                                start=(c == 0), stop=(c == 3))
                        p.inc(ins_last, "T")
                    st.m["ofm_used_T"] = p.cnt["T"]
                    wo_T = p.cnt["T"]

                    @block.vector
                    def _(e, tt=tt, dc=dc, wo_T=wo_T):
                        p.w(e, "T", wo_T)
                        xd = sb["x_tm"][:, tt * D + dc * 512: tt * D + (dc + 1) * 512]
                        ins = e.tensor_tensor(xd, xd, ps_wo[dc][:], ALU.add)
                        p.inc(ins, "V")
                    st.m[f"woev_{dc}"] = p.cnt["V"]
                    if dc == 1:
                        st.m[f"xfin_{tt}"] = p.cnt["V"]
            st.m["strip_T_end"] = p.cnt["T"]
            st.m["outc_tp_T"] = p.cnt["T"]


def emit_out(st):
    p, block = st.p, st.block

    @block.sync
    def _(e):
        for g in range(4):
            p.w(e, "V", st.m[f"xfin_{4 * g + 3}"])
            src = st.sb["x_tm"][:, 4 * g * D:(4 * g + 4) * D].rearrange(
                "p (n d) -> p n d", d=D)
            dst = st.io["out"].rearrange("(n p) d -> p n d", p=128)[:, 4 * g:4 * (g + 1), :]
            p.inc(e.dma_start(out=dst, in_=src), "O", 16)
        p.w(e, "O")


def _host_prep(inputs):
    x = np.ascontiguousarray(np.asarray(inputs["L0_attn_x"], dtype=np.float32))
    Wq = np.asarray(inputs["L0_attn_Wq"], dtype=np.float32)
    Wkv = np.ascontiguousarray(np.asarray(inputs["L0_attn_Wkv"], dtype=np.float32))
    Wk = np.ascontiguousarray(np.asarray(inputs["L0_attn_Wk"], dtype=np.float32))
    Wv = np.ascontiguousarray(np.asarray(inputs["L0_attn_Wv"], dtype=np.float32))
    Wo = np.asarray(inputs["L0_attn_Wo"], dtype=np.float32)
    q_g = np.asarray(inputs["L0_attn_q_g"], dtype=np.float32)
    k_g = np.asarray(inputs["L0_attn_k_g"], dtype=np.float32)
    gate_w = np.asarray(inputs["L0_ffn_gate_w"], dtype=np.float32)
    gate_b = np.asarray(inputs["L0_ffn_gate_bias"], dtype=np.float32)
    Wg = np.asarray(inputs["L0_ffn_Wg"], dtype=np.float32)
    Wu = np.asarray(inputs["L0_ffn_Wu"], dtype=np.float32)
    Wd = np.asarray(inputs["L0_ffn_Wd"], dtype=np.float32)
    Wsg = np.asarray(inputs["L0_ffn_Ws_g"], dtype=np.float32)
    Wsu = np.asarray(inputs["L0_ffn_Ws_u"], dtype=np.float32)
    Wsd = np.asarray(inputs["L0_ffn_Ws_d"], dtype=np.float32)

    slots = []
    for e in range(16):
        slots.append(np.concatenate([
            Wg[e].T.reshape(-1), Wu[e].T.reshape(-1), Wd[e].T.reshape(-1)]))
    for hf in range(2):
        sl = slice(hf * 512, (hf + 1) * 512)
        slots.append(np.concatenate([
            Wsg[sl, :].T.reshape(-1), Wsu[sl, :].T.reshape(-1),
            Wsd[:, sl].T.reshape(-1)]))
    wblob = np.ascontiguousarray(np.stack(slots))

    gwT = np.ascontiguousarray(gate_w.T)

    order = []
    for qc in range(4):
        for j in range(4):
            h = 4 * qc + j
            order += list(range(h * 64, h * 64 + 32))
    for qc in range(4):
        for j in range(4):
            h = 4 * qc + j
            order += list(range(h * 64 + 32, h * 64 + 64))
    wq = np.ascontiguousarray(Wq[:, order])
    wo_sub = np.ascontiguousarray(Wo.reshape(16, 64, D)[:, 0:32, :].reshape(512, D))

    cf = np.zeros((128, CF_COLS), dtype=np.float32)
    cf[:, CF_GB:CF_GB + 16] = gate_b[None, :]
    cf[:, CF_QG] = np.tile(q_g[:32], 4)
    cf[:, CF_KG] = np.tile(k_g, 4)
    pidx = np.arange(128)
    cf[0:16, CF_I16:CF_I16 + 16] = np.eye(16, dtype=np.float32)

    cb = np.zeros((128, CB_COLS), dtype=np.float32)
    cb[:, CB_ID:CB_ID + 128] = np.eye(128, dtype=np.float32)
    for c in range(8):
        head = 4 * (c % 4) + pidx // 32
        cb[pidx, CB_INDQ + c * 16 + head] = 1.0
    for c in range(4):
        head = 4 * c + pidx // 32
        cb[pidx, CB_INDK + c * 16 + head] = 1.0
    for qc in range(4):
        for h in range(4 * qc, 4 * qc + 4):
            r0 = 32 * (h % 4)
            cb[h, CB_SEL + qc * 128 + r0: CB_SEL + qc * 128 + r0 + 32] = 1.0
    ki = np.arange(128)[:, None]
    qj = np.arange(128)[None, :]
    cbm = (qj >= ki).astype(np.float32)

    shared = dict(wblob=wblob, gwT=gwT, wq=wq, wkv=Wkv, wk=Wk, wv=Wv,
                  wo=wo_sub, cf=cf, cb=cb, cbm=cbm)
    return x, shared


def kernel(**inputs):
    x, shared = _host_prep(inputs)
    nc = build_program()
    in_maps = []
    for bb in range(8):
        m = dict(shared)
        m["x"] = np.ascontiguousarray(x[bb])
        in_maps.append(m)
    res = run_bass_kernel_spmd(nc, in_maps, core_ids=list(range(8)))
    out = np.stack([np.asarray(r["out"], dtype=np.float32) for r in res.results])
    return out
